# version 11
# baseline (speedup 1.0000x reference)
import contextlib
import numpy as np

import concourse.bass as bass
import concourse.mybir as mybir
from concourse.alu_op_type import AluOpType
from concourse.bass_utils import run_bass_kernel_spmd

F32 = mybir.dt.float32
BF16 = mybir.dt.bfloat16
AF = mybir.ActivationFunctionType

D = 1024
KO = 8
H = 4096
MG = 32
T_OWN = 2048
HALF = 1024
EXT = 2176
VW = 1088
SHIFTS = [0, -32, -16, -4, -1, 1, 4, 16, 32]
NK = 9
EPS = 1e-5
FT = 256
NFT = T_OWN // FT

TE, SC, VE, GP, SY = range(5)
_QN = ["tensor", "scalar", "vector", "gpsimd", "sync"]

_DEBUG = False


NL = 8


class _Op:
    __slots__ = ("qi", "fn", "dma", "need", "deps", "val", "flushed", "tiny",
                 "sem")


class _Prog:

    def __init__(self, sems, dsems):
        self.sems = sems
        self.dsems = dsems
        self.q = [[] for _ in range(5)]
        self.cnt = [0] * 5
        self.dcnt = [[0] * NL for _ in range(5)]
        self.dseq = [0] * 5
        self.dlast = [[None] * NL for _ in range(5)]
        self.waited = [dict() for _ in range(5)]
        self.final_wait = None

    def emit(self, qi, fn, deps=(), dma=False, need=False, tiny=False):
        op = _Op()
        op.qi, op.fn, op.dma, op.need = qi, fn, dma, need
        op.deps, op.val, op.flushed = [], None, False
        op.tiny = tiny
        op.sem = None
        if dma:
            lane = self.dseq[qi] % NL
            self.dseq[qi] += 1
            self.dcnt[qi][lane] += 16
            op.sem = self.dsems[qi][lane]
            op.val = self.dcnt[qi][lane]
            op.need = True
            self.dlast[qi][lane] = op
        for d in deps:
            if d is None:
                continue
            if d.qi == qi and not d.dma and not d.tiny:
                continue
            if d.flushed:
                assert d.need, "dep on flushed op that was not need-marked"
            elif not d.dma:
                d.need = True
            op.deps.append(d)
        self.q[qi].append(op)
        return op

    def barrier(self):
        lasts = [(self.q[i][-1] if self.q[i] else None) for i in range(5)]
        dl = [op for qi in range(5) for op in self.dlast[qi] if op is not None]
        nops = []
        for qi in range(5):
            deps = [l for j, l in enumerate(lasts) if l is not None and j != qi]
            deps += dl
            nops.append(self.emit(qi, lambda e: e.nop(), deps=deps, need=True))
        return nops

    def flush_into(self, block):
        for qi in range(5):
            c = self.cnt[qi]
            for op in self.q[qi]:
                if op.dma:
                    continue
                if op.need:
                    c += 1
                op.val = c
            self.cnt[qi] = c
        pend = [list(self.q[i]) for i in range(5)]
        for i in range(5):
            self.q[i] = []
        prog = self

        def mk(qi):
            def section(e):
                for op in pend[qi]:
                    for d in op.deps:
                        w = prog.waited[qi]
                        sem = d.sem if d.dma else prog.sems[d.qi]
                        key = id(sem)
                        if w.get(key, -1) >= d.val:
                            continue
                        w[key] = d.val
                        e.wait_ge(sem, d.val)
                    inst = op.fn(e)
                    if op.need:
                        if isinstance(inst, (list, tuple)):
                            inst = inst[-1]
                        if op.dma:
                            inst.then_inc(op.sem, 16)
                        else:
                            inst.then_inc(prog.sems[qi], 1)
                    op.flushed = True
                if prog.final_wait is not None and prog.final_wait.qi == qi \
                        and prog.final_wait in pend[qi]:
                    fw = prog.final_wait
                    e.wait_ge(fw.sem if fw.dma else prog.sems[qi], fw.val)
            return section

        for qi, name in enumerate(_QN):
            getattr(block, name)(mk(qi))


def _ap(t, off, dims):
    return bass.AP(t, off, [list(d) for d in dims])


def build_nc():
    nc = bass.Bass("TRN2", target_bir_lowering=False)

    xh_d = nc.dram_tensor("xh", [EXT, D], F32, kind="ExternalInput")
    xl_d = nc.dram_tensor("xl", [96, D], F32, kind="ExternalInput")
    wg_d = nc.dram_tensor("wg", [NK * D, D], F32, kind="ExternalInput")
    wv_d = nc.dram_tensor("wv", [D, D], F32, kind="ExternalInput")
    w1_d = nc.dram_tensor("w1", [D, H], F32, kind="ExternalInput")
    w2_d = nc.dram_tensor("w2", [H, D], F32, kind="ExternalInput")
    b1_d = nc.dram_tensor("b1", [H], F32, kind="ExternalInput")
    b2_d = nc.dram_tensor("b2", [D], F32, kind="ExternalInput")
    gam_d = nc.dram_tensor("gamma", [D], F32, kind="ExternalInput")
    bet_d = nc.dram_tensor("beta", [D], F32, kind="ExternalInput")
    scl_d = nc.dram_tensor("scal", [1], F32, kind="ExternalInput")
    out_d = nc.dram_tensor("out", [T_OWN, D], F32, kind="ExternalOutput")

    ikind = "ExternalOutput" if _DEBUG else "Internal"
    x1sp = nc.dram_tensor("x1sp", [T_OWN, D], F32, kind=ikind)
    y2ls_dram = nc.dram_tensor("y2ls_dram", [16, D], BF16, kind=ikind)
    if _DEBUG:
        dbg_y1t = nc.dram_tensor("dbg_y1t", [128, KO * EXT], BF16,
                                 kind="ExternalOutput")
        dbg_v = nc.dram_tensor("dbg_v", [128, KO * VW], F32,
                               kind="ExternalOutput")
        dbg_acc = nc.dram_tensor("dbg_acc", [128, KO * HALF], F32,
                                 kind="ExternalOutput")

    with contextlib.ExitStack() as outer:
        E = outer.enter_context
        sems = [E(nc.semaphore(f"s_{n}")) for n in _QN]
        dsems = [None] * 5
        for qi in (SC, GP, SY):
            dsems[qi] = [E(nc.semaphore(f"d_{_QN[qi]}_{l}"))
                         for l in range(NL)]
        P = _Prog(sems, dsems)

        id_f = E(nc.sbuf_tensor("id_f", [128, 128], F32))
        scale_b = E(nc.sbuf_tensor("scale_b", [128, 1], F32))
        b1c = E(nc.sbuf_tensor("b1c", [128, 32], F32))
        b2c = E(nc.sbuf_tensor("b2c", [128, 8], F32))
        odd1T = E(nc.sbuf_tensor("odd1T", [128, 32], BF16))
        O_all = E(nc.sbuf_tensor("O_all", [128, 512], BF16))
        M2 = E(nc.sbuf_tensor("M2", [16, D], BF16))
        stats = E(nc.sbuf_tensor("stats", [128, 64], F32))
        bnb = E(nc.sbuf_tensor("bnb", [128, 12], F32))
        cst = E(nc.sbuf_tensor("cst", [128, 4], F32))
        cst_tok = [None]

        slot_last = [None] * 8
        slot_rr = [0]

        def rstd(src_t, nrows, deps):
            s = slot_rr[0]
            slot_rr[0] = (s + 1) % 8
            st = stats
            c0 = s * 8
            dps = list(deps)
            a1 = P.emit(VE, lambda e, t=src_t, n=nrows: e.bn_stats(
                bnb[0:n, 0:6], t[0:n, 0:512]), deps=dps, tiny=True)
            a2 = P.emit(VE, lambda e, t=src_t, n=nrows: e.bn_stats(
                bnb[0:n, 6:12], t[0:n, 512:1024]), deps=dps + [a1], tiny=True)
            sdep = [slot_last[s]] if slot_last[s] is not None else []
            b = P.emit(VE, lambda e, n=nrows, c0=c0: e.bn_aggr(
                st[0:n, c0:c0 + 2], bnb[0:n, 0:12]),
                deps=[a1, a2] + sdep, tiny=True)
            a4 = P.emit(SC, lambda e, n=nrows, c0=c0: e.activation(
                st[0:n, c0 + 2:c0 + 3], st[0:n, c0 + 1:c0 + 2], AF.Sqrt,
                bias=cst[0:n, 0:1]), deps=[b, cst_tok[0]], tiny=True)
            a5 = P.emit(VE, lambda e, n=nrows, c0=c0: e.reciprocal(
                st[0:n, c0 + 3:c0 + 4], st[0:n, c0 + 2:c0 + 3]),
                deps=[a4], tiny=True)
            a6 = P.emit(VE, lambda e, n=nrows, c0=c0: e.tensor_scalar(
                st[0:n, c0 + 4:c0 + 5], st[0:n, c0:c0 + 1],
                st[0:n, c0 + 3:c0 + 4], cst[0:n, 1:2],
                AluOpType.mult, AluOpType.mult),
                deps=[a5, b, cst_tok[0]], tiny=True)
            return st[0:nrows, c0 + 3:c0 + 4], st[0:nrows, c0 + 4:c0 + 5], a6, s

        def slot_done(s, op):
            slot_last[s] = op

        cross = {}

        with contextlib.ExitStack() as attn:
            A = attn.enter_context
            y1T = A(nc.sbuf_tensor("y1T", [128, KO * EXT], BF16))
            y2L = A(nc.sbuf_tensor("y2L", [16, D], F32))
            y2LT = A(nc.sbuf_tensor("y2LT", [128, 128], BF16))
            y2ls_sb = A(nc.sbuf_tensor("y2ls_sb", [16, D], BF16))
            acc = A(nc.sbuf_tensor("acc", [128, KO * HALF], F32))
            v_all = A(nc.sbuf_tensor("v_all", [128, KO * VW], F32))
            wgt = A(nc.sbuf_tensor("wgt", [128, 2 * KO * D], BF16))
            wvt = A(nc.sbuf_tensor("wvt", [128, KO * D], BF16))
            x0r = [A(nc.sbuf_tensor(f"x0r{i}", [128, D], F32)) for i in range(2)]
            g_buf = [A(nc.sbuf_tensor(f"gb{i}", [128, D], F32)) for i in range(2)]
            wring = [A(nc.sbuf_tensor(f"wr{i}", [128, D], F32)) for i in range(4)]
            vL = A(nc.sbuf_tensor("vL", [128, KO * 88], F32))
            accL = A(nc.sbuf_tensor("accL", [128, KO * 16], F32))
            y1LT = A(nc.sbuf_tensor("y1LT", [128, KO * 128], BF16))
            gl_sb = A(nc.sbuf_tensor("gl_sb", [128, 16], F32))
            gam_b = A(nc.sbuf_tensor("gam_b", [128, D], F32))
            bet_b = A(nc.sbuf_tensor("bet_b", [128, D], F32))
            xlt = A(nc.sbuf_tensor("xlt", [128, D], F32))
            w_z = g_buf[0]
            lwB = g_buf[1]

            p_va = A(nc.psum_tensor("p_va", [128, 512], F32))
            p_vb = A(nc.psum_tensor("p_vb", [128, 512], F32))
            p_g = [A(nc.psum_tensor(f"p_g{i}", [128, 1024], F32)) for i in range(2)]
            p_gl = A(nc.psum_tensor("p_gl", [128, 512], F32))
            p_tr = A(nc.psum_tensor("p_tr", [128, 512], F32))

            ce = P.emit(GP, lambda e: e.memset(cst[:, 0:1], EPS))
            ce2 = P.emit(GP, lambda e: e.memset(cst[:, 1:2], -1.0), deps=[ce])
            ce3 = P.emit(GP, lambda e: e.memset(cst[:, 2:3], 0.5), deps=[ce2])
            cst_tok[0] = P.emit(GP, lambda e: e.memset(cst[:, 3:4], 0.25),
                                deps=[ce3])
            idm = P.emit(GP, lambda e: e.memset(id_f[:, :], 0.0))
            ids = P.emit(GP, lambda e: e.affine_select(
                out=id_f[:, :], in_=id_f[:, :], compare_op=AluOpType.not_equal,
                fill=1.0, base=0, pattern=[[-1, 128]], channel_multiplier=1),
                deps=[idm])

            ld_sc = P.emit(SY, lambda e: e.dma_start(
                scale_b[:, :], _ap(scl_d, 0, [[0, 128], [1, 1]])), dma=True)
            ld_g = P.emit(SY, lambda e: e.dma_start(
                gam_b[:, :], _ap(gam_d, 0, [[0, 128], [1, D]])), dma=True)
            ld_b = P.emit(SY, lambda e: e.dma_start(
                bet_b[:, :], _ap(bet_d, 0, [[0, 128], [1, D]])), dma=True)
            ld_b1 = P.emit(SY, lambda e: e.dma_start(
                xlt[0:32, 0:128], _ap(b1_d, 0, [[128, 32], [1, 128]])), dma=True)
            ld_b2 = P.emit(SY, lambda e: e.dma_start(
                xlt[0:8, 256:384], _ap(b2_d, 0, [[128, 8], [1, 128]])), dma=True)

            t_b1 = P.emit(TE, lambda e: e.transpose(
                p_tr[:, 0:32], xlt[0:32, 0:128], id_f[0:32, 0:32]),
                deps=[ld_b1, ids])
            e_b1 = P.emit(VE, lambda e: e.tensor_copy(b1c[:, :], p_tr[:, 0:32]),
                          deps=[t_b1], need=True)
            t_b2 = P.emit(TE, lambda e: e.transpose(
                p_tr[:, 32:40], xlt[0:8, 256:384], id_f[0:8, 0:8]),
                deps=[ld_b2, e_b1])
            e_b2 = P.emit(VE, lambda e: e.tensor_copy(b2c[:, :], p_tr[:, 32:40]),
                          deps=[t_b2], need=True)
            ptr_free = e_b2

            wvt_ops = []
            for ko in range(KO):
                wvt_ops.append(P.emit(GP, lambda e, ko=ko: e.dma_start(
                    wvt[:, ko * D:(ko + 1) * D],
                    wv_d.ap()[ko * 128:(ko + 1) * 128, :]),
                    dma=True, need=True))
            wvt_done = wvt_ops[-1]

            mz = P.emit(GP, lambda e: e.memset(xlt[:, :], 0.0),
                        deps=[t_b1, t_b2])
            ld_xl = P.emit(SY, lambda e: e.dma_start(xlt[0:96, :], xl_d.ap()[:, :]),
                           dma=True, deps=[mz])
            r_ap, m_ap, rop, sl = rstd(xlt, 128, [ld_xl])
            nrm_xl = P.emit(SC, lambda e, r=r_ap, m=m_ap: e.activation(
                wring[2][:, :], xlt[:, :], AF.Identity, bias=m, scale=r),
                deps=[rop])
            slot_done(sl, nrm_xl)
            tl_last = None
            for hb in range(2):
                tro = []
                for j in range(4):
                    ko = hb * 4 + j
                    tro.append(P.emit(TE, lambda e, ko=ko, j=j: e.transpose(
                        p_tr[:, j * 128:(j + 1) * 128],
                        wring[2][:, ko * 128:(ko + 1) * 128], id_f[:, :]),
                        deps=[nrm_xl, ids, ptr_free]))
                ev = P.emit(VE, lambda e, hb=hb: e.tensor_copy(
                    _ap(y1LT, hb * 4 * 128, [[KO * 128, 128], [128, 4], [1, 128]]),
                    _ap(p_tr, 0, [[512, 128], [128, 4], [1, 128]])),
                    deps=[tro[-1]], need=True)
                ptr_free = ev
                tl_last = ev
            y1LT_done = tl_last
            wring_free = [None, None, tl_last, None]

            x0r_free = [None, None]
            y1t_ev = [None] * 17

            def p1_tile(i):
                nonlocal ptr_free
                sl_i = i % 2
                wsl = i % 4
                ld = P.emit(SY, lambda e, i=i, s=sl_i: e.dma_start(
                    x0r[s][:, :], xh_d.ap()[i * 128:(i + 1) * 128, :]),
                    dma=True, deps=[x0r_free[sl_i]])
                r_ap, m_ap, rop, sl = rstd(x0r[sl_i], 128, [ld])
                nrm = P.emit(SC, lambda e, s=sl_i, w=wsl, r=r_ap, m=m_ap:
                             e.activation(wring[w][:, :], x0r[s][:, :],
                                          AF.Identity, bias=m, scale=r),
                             deps=[rop, wring_free[wsl]])
                slot_done(sl, nrm)
                x0r_free[sl_i] = nrm
                last_ev = None
                for hb in range(2):
                    tro = []
                    for j in range(4):
                        ko = hb * 4 + j
                        tro.append(P.emit(TE, lambda e, w=wsl, ko=ko, j=j:
                                          e.transpose(
                            p_tr[:, j * 128:(j + 1) * 128],
                            wring[w][:, ko * 128:(ko + 1) * 128], id_f[:, :]),
                            deps=[nrm, ptr_free]))
                    ev = P.emit(VE, lambda e, hb=hb, i=i: e.tensor_copy(
                        _ap(y1T, hb * 4 * EXT + i * 128,
                            [[KO * EXT, 128], [EXT, 4], [1, 128]]),
                        _ap(p_tr, 0, [[512, 128], [128, 4], [1, 128]])),
                        deps=[tro[-1]], need=True)
                    ptr_free = ev
                    last_ev = ev
                wring_free[wsl] = last_ev
                y1t_ev[i] = last_ev

            for i in range(9):
                p1_tile(i)

            def xbar_wg(k, dep_free):
                pk = k % 2
                ops = []
                for ko in range(KO):
                    ops.append(P.emit(GP, lambda e, k=k, ko=ko, pk=pk: e.dma_start(
                        wgt[:, pk * KO * D + ko * D:pk * KO * D + (ko + 1) * D],
                        wg_d.ap()[k * D + ko * 128:k * D + (ko + 1) * 128, :]),
                        dma=True, deps=[dep_free], need=True))
                return ops[-1]

            acc_free = None
            accL_done = None
            tt_last = None
            v_evac = [None] * KO
            vL_evac = [None] * KO
            gbuf_free = [None, None]
            sig_last = [None, None]
            pva_free, pvb_free = None, None
            wg_free = [None, None]
            x1sp_ops = []
            o_last = od_last = st_ls = None
            m2g = []

            def v_emit(h, tt_gate):
                nonlocal pva_free, pvb_free
                chunks = [(0, 512), (512, 512), (1024, 64)]
                v_par = 0
                for dg in range(KO):
                    my_evac = None
                    for (coff, cw) in chunks:
                        ps = p_va if v_par == 0 else p_vb
                        psfree = pva_free if v_par == 0 else pvb_free
                        mms = []
                        for ko in range(KO):
                            mms.append(P.emit(TE, lambda e, ps=ps, ko=ko, dg=dg,
                                              h=h, coff=coff, cw=cw: e.matmul(
                                ps[:, 0:cw],
                                wvt[:, ko * D + dg * 128:ko * D + (dg + 1) * 128],
                                y1T[:, ko * EXT + 32 + h * HALF + coff:
                                    ko * EXT + 32 + h * HALF + coff + cw],
                                start=(ko == 0), stop=(ko == KO - 1)),
                                deps=([wvt_done, psfree, y1t_ev[min(
                                    16, (32 + h * HALF + coff + cw - 1) // 128)]]
                                      if ko == 0 else [])))
                        ev = P.emit(SC, lambda e, ps=ps, dg=dg, coff=coff, cw=cw:
                                    e.copy(v_all[:, dg * VW + coff:
                                                 dg * VW + coff + cw],
                                           ps[:, 0:cw]),
                                    deps=[mms[-1], tt_gate], need=True)
                        my_evac = ev
                        if v_par == 0:
                            pva_free = ev
                        else:
                            pvb_free = ev
                        v_par ^= 1
                    v_evac[dg] = my_evac
                    if h == 0:
                        ps = p_va if v_par == 0 else p_vb
                        psfree = pva_free if v_par == 0 else pvb_free
                        mms = []
                        for ko in range(KO):
                            mms.append(P.emit(TE, lambda e, ps=ps, ko=ko, dg=dg:
                                              e.matmul(
                                ps[:, 0:96],
                                wvt[:, ko * D + dg * 128:ko * D + (dg + 1) * 128],
                                y1LT[:, ko * 128:ko * 128 + 96],
                                start=(ko == 0), stop=(ko == KO - 1)),
                                deps=([y1LT_done, psfree] if ko == 0 else [])))
                        ev = P.emit(SC, lambda e, ps=ps, dg=dg: e.copy(
                            vL[:, dg * 88:(dg + 1) * 88], ps[:, 1:89]),
                            deps=[mms[-1]], need=True)
                        vL_evac[dg] = ev
                        if v_par == 0:
                            pva_free = ev
                        else:
                            pvb_free = ev
                        v_par ^= 1

            for h in range(2):
                if h == 0:
                    v_emit(0, None)
                    for i in range(9, 17):
                        p1_tile(i)

                xb = [xbar_wg(0, wg_free[0]), xbar_wg(1, wg_free[1])]
                for k in range(NK):
                    pk = k % 2
                    s_k = SHIFTS[k]
                    for dg in range(KO):
                        pg = dg % 2
                        mms = []
                        for ko in range(KO):
                            for nch in range(2):
                                mms.append(P.emit(TE, lambda e, pg=pg, pk=pk,
                                                  ko=ko, dg=dg, h=h, nch=nch:
                                                  e.matmul(
                                    p_g[pg][:, nch * 512:(nch + 1) * 512],
                                    wgt[:, pk * KO * D + ko * D + dg * 128:
                                        pk * KO * D + ko * D + (dg + 1) * 128],
                                    y1T[:, ko * EXT + 64 + h * HALF + nch * 512:
                                        ko * EXT + 64 + h * HALF +
                                        (nch + 1) * 512],
                                    start=(ko == 0), stop=(ko == KO - 1)),
                                    deps=([xb[pk], sig_last[pg],
                                           y1t_ev[8 if h == 0 else 16]]
                                          if (ko == 0 and nch == 0) else [])))
                        wg_free[pk] = mms[-1]
                        sig = P.emit(SC, lambda e, pg=pg: e.activation(
                            g_buf[pg][:, :], p_g[pg][:, :], AF.Sigmoid),
                            deps=[mms[-1], gbuf_free[pg]], need=True)
                        sig_last[pg] = sig
                        if k == 0:
                            t2 = P.emit(VE, lambda e, pg=pg, dg=dg, s=s_k:
                                        e.tensor_tensor(
                                acc[:, dg * HALF:(dg + 1) * HALF],
                                g_buf[pg][:, :],
                                v_all[:, dg * VW + 32 + s:
                                       dg * VW + 32 + s + HALF],
                                AluOpType.mult),
                                deps=[sig, v_evac[dg], acc_free])
                        else:
                            t1 = P.emit(VE, lambda e, pg=pg, dg=dg, s=s_k:
                                        e.tensor_tensor(
                                g_buf[pg][:, :], g_buf[pg][:, :],
                                v_all[:, dg * VW + 32 + s:
                                       dg * VW + 32 + s + HALF],
                                AluOpType.mult),
                                deps=[sig, v_evac[dg]])
                            t2 = P.emit(VE, lambda e, pg=pg, dg=dg:
                                        e.tensor_tensor(
                                acc[:, dg * HALF:(dg + 1) * HALF],
                                acc[:, dg * HALF:(dg + 1) * HALF],
                                g_buf[pg][:, :], AluOpType.add), deps=[t1])
                        tt_last = t2
                        gbuf_free[pg] = t2
                        if h == 0:
                            lm = []
                            for ko in range(KO):
                                lm.append(P.emit(TE, lambda e, pk=pk, ko=ko,
                                                 dg=dg: e.matmul(
                                    p_gl[:, 0:16],
                                    wgt[:, pk * KO * D + ko * D + dg * 128:
                                        pk * KO * D + ko * D + (dg + 1) * 128],
                                    y1LT[:, ko * 128 + 33:ko * 128 + 49],
                                    start=(ko == 0), stop=(ko == KO - 1)),
                                    deps=([accL_done, y1LT_done]
                                          if ko == 0 else [])))
                            sigL = P.emit(SC, lambda e: e.activation(
                                gl_sb[:, :], p_gl[:, 0:16], AF.Sigmoid),
                                deps=[lm[-1], accL_done], need=True)
                            pgl_free = sigL
                            if k == 0:
                                tL = P.emit(VE, lambda e, dg=dg, s=s_k:
                                            e.tensor_tensor(
                                    accL[:, dg * 16:(dg + 1) * 16], gl_sb[:, :],
                                    vL[:, dg * 88 + 32 + s:dg * 88 + 48 + s],
                                    AluOpType.mult),
                                    deps=[sigL, vL_evac[dg]], tiny=True)
                            else:
                                tLa = P.emit(VE, lambda e, dg=dg, s=s_k:
                                             e.tensor_tensor(
                                    gl_sb[:, :], gl_sb[:, :],
                                    vL[:, dg * 88 + 32 + s:dg * 88 + 48 + s],
                                    AluOpType.mult),
                                    deps=[sigL, vL_evac[dg]], tiny=True)
                                tL = P.emit(VE, lambda e, dg=dg: e.tensor_tensor(
                                    accL[:, dg * 16:(dg + 1) * 16],
                                    accL[:, dg * 16:(dg + 1) * 16],
                                    gl_sb[:, :], AluOpType.add),
                                    deps=[tLa, accL_done], tiny=True)
                            accL_done = tL
                    if k + 2 < NK:
                        xb[(k + 2) % 2] = xbar_wg(k + 2, wg_free[(k + 2) % 2])

                if h == 0:
                    evs = []
                    for hb in range(2):
                        tro = []
                        for j in range(4):
                            dg = hb * 4 + j
                            tro.append(P.emit(TE, lambda e, dg=dg, j=j:
                                              e.transpose(
                                p_tr[0:16, j * 128:(j + 1) * 128],
                                accL[:, dg * 16:(dg + 1) * 16], id_f[:, :]),
                                deps=[accL_done, ptr_free]))
                        ev = P.emit(VE, lambda e, hb=hb: e.tensor_copy(
                            _ap(xlt, hb * 512, [[D, 16], [128, 4], [1, 128]]),
                            _ap(p_tr, 0, [[512, 16], [128, 4], [1, 128]])),
                            deps=[tro[-1]], need=True, tiny=True)
                        ptr_free = ev
                        evs.append(ev)
                    accLnat = evs[-1]
                    ld_x0L = P.emit(SY, lambda e: e.dma_start(
                        x0r[0][0:16, :], xl_d.ap()[33:49, :]), dma=True,
                        deps=[x0r_free[0]])
                    r1, n1, rop1, sl1 = rstd(xlt, 16, [accLnat])
                    nL = P.emit(SC, lambda e, r=r1, m=n1: e.activation(
                        lwB[0:16, :], xlt[0:16, :], AF.Identity, bias=m, scale=r),
                        deps=[rop1, accLnat, gbuf_free[1]])
                    slot_done(sl1, nL)
                    P.emit(VE, lambda e: e.tensor_tensor(
                        lwB[0:16, :], lwB[0:16, :], gam_b[0:16, :],
                        AluOpType.mult), deps=[nL, ld_g])
                    zL2 = P.emit(VE, lambda e: e.tensor_tensor(
                        lwB[0:16, :], lwB[0:16, :], bet_b[0:16, :],
                        AluOpType.add), deps=[ld_b])
                    r2, n2, rop2, sl2 = rstd(lwB, 16, [zL2])
                    n2L = P.emit(SC, lambda e, r=r2, m=n2: e.activation(
                        xlt[0:16, :], lwB[0:16, :], AF.Identity, bias=m, scale=r),
                        deps=[rop2])
                    slot_done(sl2, n2L)
                    gbuf_free[1] = n2L
                    x1L = P.emit(VE, lambda e: e.tensor_tensor(
                        xlt[0:16, :], xlt[0:16, :], x0r[0][0:16, :],
                        AluOpType.add),
                        deps=[n2L, ld_x0L])
                    x0r_free[0] = x1L
                    r3, n3, rop3, sl3 = rstd(xlt, 16, [x1L])
                    y2L_op = P.emit(SC, lambda e, r=r3, m=n3: e.activation(
                        y2L[:, :], xlt[0:16, :], AF.Identity, bias=m, scale=r),
                        deps=[rop3, x1L])
                    slot_done(sl3, y2L_op)
                    tro = []
                    for ko in range(KO):
                        tro.append(P.emit(TE, lambda e, ko=ko: e.transpose(
                            p_tr[:, ko * 16:(ko + 1) * 16],
                            y2L[:, ko * 128:(ko + 1) * 128], id_f[0:16, 0:16]),
                            deps=[y2L_op, ptr_free]))
                    y2LT_op = P.emit(VE, lambda e: e.tensor_copy(
                        y2LT[:, :], p_tr[:, 0:128]), deps=[tro[-1]], need=True,
                        tiny=True)
                    ptr_free = y2LT_op
                    y2ls_op = P.emit(SC, lambda e: e.activation(
                        y2ls_sb[:, :], y2L[:, :], AF.Identity,
                        scale=scale_b[0:16, 0:1]), deps=[y2L_op, ld_sc])
                    st_ls = P.emit(SY, lambda e: e.dma_start(
                        y2ls_dram.ap(), y2ls_sb[:, :]), dma=True,
                        deps=[y2ls_op], need=True)
                    m2z = P.emit(GP, lambda e: e.memset(M2[:, :], 0.0))
                    for j in range(4):
                        for pm in range(4):
                            m2g.append(P.emit(GP, lambda e, j=j, pm=pm:
                                              e.dma_start(
                                M2[j * 4 + pm:j * 4 + pm + 1,
                                   j * 256:(j + 1) * 256],
                                _ap(y2ls_dram, (8 + 2 * j) * D + pm,
                                    [[1, 1], [4, 256]]),
                                allow_slow_non_contiguous=True),
                                dma=True, deps=[st_ls, m2z], need=True))
                    oz = P.emit(VE, lambda e: e.memset(O_all[:, :], 0.0),
                                deps=[y2LT_op], tiny=True)
                    for g in range(32):
                        pm, ko = g // 8, g % 8
                        o_last = P.emit(VE, lambda e, g=g, pm=pm, ko=ko:
                                        e.tensor_copy(
                            _ap(O_all, g * 16 + pm, [[512, 128], [4, 4]]),
                            _ap(y2LT, ko * 16 + 9, [[128, 128], [2, 4]])),
                            deps=[oz, y2LT_op], need=(g == 31))
                    for ko in range(KO):
                        od_last = P.emit(VE, lambda e, ko=ko: e.tensor_copy(
                            _ap(odd1T, ko * 4, [[32, 128], [1, 4]]),
                            _ap(y2LT, ko * 16 + 1, [[128, 128], [2, 4]])),
                            deps=[y2LT_op], need=(ko == KO - 1))
                    v_emit(1, tt_last)

                tt_final = tt_last
                tr_last = None
                wz_free = [gbuf_free[0], gbuf_free[1]]
                if h == 0:
                    ering = wring + x0r
                    ering_free = [wring_free[0], wring_free[1], wring_free[2],
                                  wring_free[3], x0r_free[0], x0r_free[1]]
                for i in range(8):
                    wsl = i % 6
                    tro0 = []
                    for j in range(4):
                        tro0.append(P.emit(TE, lambda e, i=i, j=j:
                                           e.transpose(
                            p_tr[:, j * 128:(j + 1) * 128],
                            acc[:, j * HALF + i * 128:
                                j * HALF + (i + 1) * 128],
                            id_f[:, :]),
                            deps=[tt_final, ptr_free]))
                    ev0 = P.emit(VE, lambda e, w=wsl: e.tensor_copy(
                        ering[w][:, 0:512], p_tr[:, 0:512]),
                        deps=[tro0[-1], ering_free[wsl]], need=True, tiny=True)
                    ptr_free = ev0
                    tro1 = []
                    for j in range(4):
                        dg = 4 + j
                        tro1.append(P.emit(TE, lambda e, dg=dg, i=i, j=j:
                                           e.transpose(
                            p_gl[:, j * 128:(j + 1) * 128],
                            acc[:, dg * HALF + i * 128:
                                dg * HALF + (i + 1) * 128],
                            id_f[:, :]),
                            deps=[tt_final, pgl_free]))
                    tr_last = tro1[-1]
                    ev1 = P.emit(SC, lambda e, w=wsl: e.copy(
                        ering[w][:, 512:1024], p_gl[:, 0:512]),
                        deps=[tro1[-1], ering_free[wsl]], need=True, tiny=True)
                    pgl_free = ev1
                    natacc = ev1
                    wz = g_buf[i % 2]
                    r1, n1, rop1, sl1 = rstd(ering[wsl], 128, [ev0, ev1])
                    nrm1 = P.emit(SC, lambda e, wz=wz, w=wsl, r=r1, m=n1:
                                  e.activation(
                        wz[:, :], ering[w][:, :], AF.Identity, bias=m, scale=r),
                        deps=[rop1, ev0, ev1, wz_free[i % 2]])
                    slot_done(sl1, nrm1)
                    gm = P.emit(GP, lambda e, wz=wz: e.tensor_mul(
                        wz[:, :], wz[:, :], gam_b[:, :]),
                        deps=[nrm1, ld_g], need=True)
                    z2 = P.emit(GP, lambda e, wz=wz: e.tensor_add(
                        wz[:, :], wz[:, :], bet_b[:, :]),
                        deps=[gm, ld_b], need=True)
                    r2, n2, rop2, sl2 = rstd(wz, 128, [z2])
                    nrm2 = P.emit(SC, lambda e, wz=wz, w=wsl, r=r2, m=n2:
                                  e.activation(
                        ering[w][:, :], wz[:, :], AF.Identity, bias=m, scale=r),
                        deps=[rop2])
                    slot_done(sl2, nrm2)
                    wz_free[i % 2] = nrm2
                    sp = P.emit(SY, lambda e, h=h, i=i, w=wsl: e.dma_start(
                        x1sp.ap()[h * HALF + i * 128:h * HALF + (i + 1) * 128, :],
                        ering[w][:, :]), dma=True, deps=[nrm2], need=True)
                    x1sp_ops.append(sp)
                    ering_free[wsl] = sp
                acc_free = tr_last
                gbuf_free[0] = wz_free[0]
                gbuf_free[1] = wz_free[1]

            spill_done = x1sp_ops[-1]
            if _DEBUG:
                P.emit(SY, lambda e: e.dma_start(dbg_y1t.ap(), y1T[:, :]),
                       dma=True, deps=[y1t_ev[16]], need=True)
                P.emit(SY, lambda e: e.dma_start(dbg_v.ap(), v_all[:, :]),
                       dma=True, deps=[v_evac[KO - 1]], need=True)
                P.emit(SY, lambda e: e.dma_start(dbg_acc.ap(), acc[:, :]),
                       dma=True, deps=[tt_last], need=True)
            bar = P.barrier()
            cross.update(m2g=m2g,
                         o_last=o_last, od_last=od_last, st_ls=st_ls,
                         e_b1=e_b1, e_b2=e_b2, spill_done=spill_done, bar=bar,
                         x1sp_ops=x1sp_ops)

            blk1 = attn.enter_context(nc.Block())
            P.flush_into(blk1)

        for s in range(8):
            slot_last[s] = None

        with contextlib.ExitStack() as ffn:
            B = ffn.enter_context
            w1t = B(nc.sbuf_tensor("w1t", [128, KO * H], BF16))
            w2t = B(nc.sbuf_tensor("w2t", [128, MG * D], BF16))
            h_bf = B(nc.sbuf_tensor("h_bf", [128, MG * FT], BF16))
            y2t = [B(nc.sbuf_tensor(f"y2t{i}", [128, KO * FT], BF16))
                   for i in range(2)]
            x1r = [B(nc.sbuf_tensor(f"x1r{i}", [128, D], F32)) for i in range(4)]
            wf = [B(nc.sbuf_tensor(f"wf{i}", [128, D], F32)) for i in range(4)]
            w_odt = B(nc.sbuf_tensor("w_odt", [128, KO * FT], F32))
            M1 = B(nc.sbuf_tensor("M1", [4, H], BF16))
            c_bf = B(nc.sbuf_tensor("c_bf", [4, FT], BF16))
            t_bf = B(nc.sbuf_tensor("t_bf", [16, FT], BF16))

            p_c = B(nc.psum_tensor("p_c", [4, 512], F32))
            p_t = B(nc.psum_tensor("p_t", [16, 512], F32))
            p_h = [B(nc.psum_tensor(f"p_h{i}", [128, 512], F32))
                   for i in range(2)]
            p_o = [B(nc.psum_tensor(f"p_o{i}", [128, 512], F32))
                   for i in range(2)]
            p_f = B(nc.psum_tensor("p_f", [128, 1024], F32))

            bar = cross["bar"]
            bsy, bgp = bar[SY], bar[GP]
            w1t_ops = []
            for g in range(8):
                w1t_ops.append(P.emit(GP, lambda e, g=g: e.dma_start(
                    _ap(w1t, g * 512, [[KO * H, 128], [H, KO], [1, 512]]),
                    _ap(w1_d, g * 512, [[H, 128], [128 * H, KO], [1, 512]])),
                    dma=True, deps=[bgp], need=True))
            w2t_ops = []
            for ko in range(MG):
                w2t_ops.append(P.emit(GP, lambda e, ko=ko: e.dma_start(
                    w2t[:, ko * D:(ko + 1) * D],
                    w2_d.ap()[ko * 128:(ko + 1) * 128, :]),
                    dma=True, deps=[bgp], need=True))
            m1z = P.emit(GP, lambda e: e.memset(M1[:, :], 0.0), deps=[bgp])
            m1_done = None
            for j in range(4):
                m1_done = P.emit(SY, lambda e, j=j: e.dma_start(
                    M1[j:j + 1, j * D:(j + 1) * D],
                    y2ls_dram.ap()[2 * j:2 * j + 1, :]),
                    dma=True, deps=[m1z, cross["st_ls"], bsy], need=True)

            ptr2_free = bar[VE]
            x1r_free = [None] * 4
            wf_free = [None] * 4
            y2t_free = [None, None]
            hbf_free = None
            codt_free = None
            gel_prev = [None, None]
            oev_prev = [None, None]
            prev_cev = None
            prev_tev = None
            cbf_free = None
            tbf_free = None
            out_ops = []

            def pro(i):
                sub_nrm = []
                adds = []
                for sub in range(2):
                    s = (2 * i + sub) % 4
                    wslot = (2 * i + sub) % 4
                    lda = P.emit(SY, lambda e, i=i, sub=sub, s=s: e.dma_start(
                        x1r[s][:, :],
                        x1sp.ap()[i * FT + sub * 128:i * FT + sub * 128 + 128, :]),
                        dma=True, deps=[x1r_free[s],
                                        cross["x1sp_ops"][2 * i + sub]])
                    ldx = P.emit(SY, lambda e, i=i, sub=sub, w=wslot:
                                 e.dma_start(
                        wf[w][:, :],
                        xh_d.ap()[64 + i * FT + sub * 128:
                                  64 + i * FT + sub * 128 + 128, :]),
                        dma=True, deps=[wf_free[wslot]])
                    xadd = P.emit(VE, lambda e, s=s, w=wslot: e.tensor_tensor(
                        x1r[s][:, :], x1r[s][:, :], wf[w][:, :],
                        AluOpType.add), deps=[lda, ldx])
                    adds.append(xadd)
                    r_ap, m_ap, rop, sl = rstd(x1r[s], 128, [xadd])
                    nrm = P.emit(SC, lambda e, s=s, w=wslot, r=r_ap, m=m_ap:
                                 e.activation(wf[w][:, :], x1r[s][:, :],
                                              AF.Identity, bias=m, scale=r),
                                 deps=[rop, xadd])
                    slot_done(sl, nrm)
                    sub_nrm.append((nrm, wslot))
                return {"lds": adds, "sub_nrm": sub_nrm}

            pro_d = [None] * NFT
            pro_d[0] = pro(0)
            for i in range(NFT):
                yp = i % 2
                lds = pro_d[i]["lds"]
                tr_ev = []
                for sub in range(2):
                    nrm, wslot = pro_d[i]["sub_nrm"][sub]
                    tro = []
                    for ko in range(KO):
                        tro.append(P.emit(TE, lambda e, w=wslot, ko=ko:
                                          e.transpose(
                            p_f[:, ko * 128:(ko + 1) * 128],
                            wf[w][:, ko * 128:(ko + 1) * 128], id_f[:, :]),
                            deps=[nrm, ptr2_free, y2t_free[yp]]))
                    ev = P.emit(SC, lambda e, yp=yp, sub=sub: e.copy(
                        _ap(y2t[yp], sub * 128,
                            [[KO * FT, 128], [FT, KO], [1, 128]]),
                        _ap(p_f, 0, [[1024, 128], [128, KO], [1, 128]])),
                        deps=[tro[-1]], need=True)
                    ptr2_free = ev
                    tr_ev.append(ev)
                    wf_free[wslot] = ev
                y2t_done = tr_ev[-1]
                cms = []
                for ko in range(KO):
                    cms.append(P.emit(TE, lambda e, yp=yp, ko=ko: e.matmul(
                        p_c[:, 0:FT], odd1T[:, ko * 4:(ko + 1) * 4],
                        y2t[yp][:, ko * FT:(ko + 1) * FT],
                        start=(ko == 0), stop=(ko == KO - 1)),
                        deps=([y2t_done, cross["od_last"], prev_cev]
                              if ko == 0 else [])))
                cev = P.emit(VE, lambda e: e.tensor_copy(c_bf[:, :],
                                                         p_c[:, 0:FT]),
                             deps=[cms[-1], cbf_free], need=True)
                prev_cev = cev
                gel_last = None
                hms = []
                for mg in range(MG):
                    hp = mg % 2
                    hms = []
                    for ko in range(KO):
                        hms.append(P.emit(TE, lambda e, hp=hp, yp=yp, ko=ko,
                                          mg=mg: e.matmul(
                            p_h[hp][:, 0:FT],
                            w1t[:, ko * H + mg * 128:ko * H + (mg + 1) * 128],
                            y2t[yp][:, ko * FT:(ko + 1) * FT],
                            start=(ko == 0), stop=False),
                            deps=([w1t_ops[mg // 4], y2t_done, gel_prev[hp]]
                                  if ko == 0 else [])))
                    hms.append(P.emit(TE, lambda e, hp=hp, mg=mg: e.matmul(
                        p_h[hp][:, 0:FT], M1[:, mg * 128:(mg + 1) * 128],
                        c_bf[:, :], start=False, stop=True),
                        deps=[cev, m1_done]))
                    gel = P.emit(SC, lambda e, hp=hp, mg=mg: e.activation(
                        h_bf[:, mg * FT:(mg + 1) * FT], p_h[hp][:, 0:FT],
                        AF.Gelu, bias=b1c[:, mg:mg + 1]),
                        deps=[hms[-1], cross["e_b1"], hbf_free], need=True)
                    gel_prev[hp] = gel
                    gel_last = gel
                cbf_free = hms[-1]
                if i + 1 < NFT:
                    pro_d[i + 1] = pro(i + 1)
                tms = []
                for g in range(MG):
                    tms.append(P.emit(TE, lambda e, g=g: e.matmul(
                        p_t[:, 0:FT], O_all[:, g * 16:(g + 1) * 16],
                        h_bf[:, g * FT:(g + 1) * FT],
                        start=(g == 0), stop=(g == MG - 1)),
                        deps=([gel_last, cross["o_last"], prev_tev]
                              if g == 0 else [])))
                tev = P.emit(VE, lambda e: e.tensor_copy(t_bf[:, :],
                                                         p_t[:, 0:FT]),
                             deps=[tms[-1], tbf_free], need=True)
                prev_tev = tev
                oev_last = None
                oms = []
                for dg in range(KO):
                    op_ = dg % 2
                    oms = []
                    for ko in range(MG):
                        oms.append(P.emit(TE, lambda e, op_=op_, ko=ko, dg=dg:
                                          e.matmul(
                            p_o[op_][:, 0:FT],
                            w2t[:, ko * D + dg * 128:ko * D + (dg + 1) * 128],
                            h_bf[:, ko * FT:(ko + 1) * FT],
                            start=(ko == 0), stop=False),
                            deps=([w2t_ops[ko], gel_last, oev_prev[op_]]
                                  if ko == 0 else [w2t_ops[ko]])))
                    oms.append(P.emit(TE, lambda e, op_=op_, dg=dg: e.matmul(
                        p_o[op_][:, 0:FT], M2[:, dg * 128:(dg + 1) * 128],
                        t_bf[:, :], start=False, stop=True),
                        deps=[tev] + cross["m2g"]))
                    oev = P.emit(SC, lambda e, op_=op_, dg=dg: e.activation(
                        w_odt[:, dg * FT:(dg + 1) * FT], p_o[op_][:, 0:FT],
                        AF.Identity, bias=b2c[:, dg:dg + 1]),
                        deps=[oms[-1], cross["e_b2"], codt_free], need=True)
                    oev_prev[op_] = oev
                    oev_last = oev
                hbf_free = oms[-1]
                tbf_free = oms[-1]
                y2t_free[yp] = hms[-1]
                tro = []
                for sub in range(2):
                    tro = []
                    for dg in range(KO):
                        tro.append(P.emit(TE, lambda e, dg=dg, sub=sub:
                                          e.transpose(
                            p_f[:, dg * 128:(dg + 1) * 128],
                            w_odt[:, dg * FT + sub * 128:
                                  dg * FT + sub * 128 + 128],
                            id_f[:, :]),
                            deps=[oev_last, ptr2_free]))
                    wslot = (2 * i + sub) % 4
                    ev = P.emit(VE, lambda e, w=wslot: e.tensor_copy(
                        wf[w][:, :], p_f[:, :]),
                        deps=[tro[-1], wf_free[wslot]], need=True, tiny=True)
                    ptr2_free = ev
                    r_ap, m_ap, rop, sl = rstd(wf[wslot], 128, [ev])
                    nrm = P.emit(SC, lambda e, w=wslot, r=r_ap, m=m_ap:
                                 e.activation(wf[w][:, :], wf[w][:, :],
                                              AF.Identity, bias=m, scale=r),
                                 deps=[rop, ev])
                    slot_done(sl, nrm)
                    xsl = (2 * i + sub) % 4
                    res = P.emit(VE, lambda e, w=wslot, xs=xsl: e.tensor_tensor(
                        wf[w][:, :], wf[w][:, :], x1r[xs][:, :], AluOpType.add),
                        deps=[nrm, lds[sub]])
                    st = P.emit(SY, lambda e, i=i, sub=sub, w=wslot:
                                e.dma_start(
                        out_d.ap()[i * FT + sub * 128:
                                   i * FT + sub * 128 + 128, :],
                        wf[w][:, :]), dma=True, deps=[res], need=True)
                    out_ops.append(st)
                    wf_free[wslot] = st
                    x1r_free[xsl] = res
                codt_free = tro[-1]

            P.final_wait = out_ops[-1]
            blk2 = ffn.enter_context(nc.Block())
            P.flush_into(blk2)
    return nc


def kernel(**inputs):
    x = np.asarray(inputs["L0_attn_x"], dtype=np.float32)
    wg0 = np.asarray(inputs["L0_attn_gate_W"], np.float32)
    wg = np.ascontiguousarray(
        wg0.reshape(NK, D, D).transpose(0, 2, 1)).reshape(NK * D, D)
    wv = np.ascontiguousarray(np.asarray(inputs["L0_attn_value_W"], np.float32).T)
    gam = np.ascontiguousarray(np.asarray(inputs["L0_attn_gamma"], np.float32))
    bet = np.ascontiguousarray(np.asarray(inputs["L0_attn_beta"], np.float32))
    w1 = np.ascontiguousarray(np.asarray(inputs["L0_ffn_W1"], np.float32).T)
    b1 = np.ascontiguousarray(np.asarray(inputs["L0_ffn_b1"], np.float32))
    w2 = np.ascontiguousarray(np.asarray(inputs["L0_ffn_W2"], np.float32).T)
    b2 = np.ascontiguousarray(np.asarray(inputs["L0_ffn_b2"], np.float32))
    scl = np.asarray(inputs["L0_ffn_scale"], dtype=np.float32).reshape(1)

    nc = build_nc()
    L = 8192
    in_maps = []
    for c in range(8):
        b, t0 = c // 4, (c % 4) * T_OWN
        idx = np.arange(t0 - 64, t0 + T_OWN + 64) % L
        xh = np.ascontiguousarray(x[b][idx])
        xl = np.ascontiguousarray(x[b][np.r_[L - 32:L, 0:64]])
        in_maps.append({
            "xh": xh, "xl": xl, "wg": wg, "wv": wv, "w1": w1, "w2": w2,
            "b1": b1, "b2": b2, "gamma": gam, "beta": bet, "scal": scl,
        })
    res = run_bass_kernel_spmd(nc, in_maps, core_ids=list(range(8)))
    out = np.empty((2, L, D), np.float32)
    for c, r in enumerate(res.results):
        b, t0 = c // 4, (c % 4) * T_OWN
        out[b, t0:t0 + T_OWN] = np.asarray(r["out"])
    return out


# version 12
# speedup vs baseline: 1.0273x; 1.0273x over previous
import contextlib
import numpy as np

import concourse.bass as bass
import concourse.mybir as mybir
from concourse.alu_op_type import AluOpType
from concourse.bass_utils import run_bass_kernel_spmd

F32 = mybir.dt.float32
BF16 = mybir.dt.bfloat16
AF = mybir.ActivationFunctionType

D = 1024
KO = 8
H = 4096
MG = 32
T_OWN = 2048
HALF = 1024
EXT = 2176
VW = 1088
SHIFTS = [0, -32, -16, -4, -1, 1, 4, 16, 32]
NK = 9
EPS = 1e-5
FT = 256
NFT = T_OWN // FT

TE, SC, VE, GP, SY = range(5)
_QN = ["tensor", "scalar", "vector", "gpsimd", "sync"]

_DEBUG = False


NL = 8


class _Op:
    __slots__ = ("qi", "fn", "dma", "need", "deps", "val", "flushed", "tiny",
                 "sem")


class _Prog:

    def __init__(self, sems, dsems):
        self.sems = sems
        self.dsems = dsems
        self.q = [[] for _ in range(5)]
        self.cnt = [0] * 5
        self.dcnt = [[0] * NL for _ in range(5)]
        self.dseq = [0] * 5
        self.dlast = [[None] * NL for _ in range(5)]
        self.waited = [dict() for _ in range(5)]
        self.final_wait = None

    def emit(self, qi, fn, deps=(), dma=False, need=False, tiny=False):
        op = _Op()
        op.qi, op.fn, op.dma, op.need = qi, fn, dma, need
        op.deps, op.val, op.flushed = [], None, False
        op.tiny = tiny
        op.sem = None
        if dma:
            lane = self.dseq[qi] % NL
            self.dseq[qi] += 1
            self.dcnt[qi][lane] += 16
            op.sem = self.dsems[qi][lane]
            op.val = self.dcnt[qi][lane]
            op.need = True
            self.dlast[qi][lane] = op
        for d in deps:
            if d is None:
                continue
            if d.qi == qi and not d.dma and not d.tiny:
                continue
            if d.flushed:
                assert d.need, "dep on flushed op that was not need-marked"
            elif not d.dma:
                d.need = True
            op.deps.append(d)
        self.q[qi].append(op)
        return op

    def barrier(self):
        lasts = [(self.q[i][-1] if self.q[i] else None) for i in range(5)]
        dl = [op for qi in range(5) for op in self.dlast[qi] if op is not None]
        nops = []
        for qi in range(5):
            deps = [l for j, l in enumerate(lasts) if l is not None and j != qi]
            deps += dl
            nops.append(self.emit(qi, lambda e: e.nop(), deps=deps, need=True))
        return nops

    def flush_into(self, block):
        for qi in range(5):
            c = self.cnt[qi]
            for op in self.q[qi]:
                if op.dma:
                    continue
                if op.need:
                    c += 1
                op.val = c
            self.cnt[qi] = c
        pend = [list(self.q[i]) for i in range(5)]
        for i in range(5):
            self.q[i] = []
        prog = self

        def mk(qi):
            def section(e):
                for op in pend[qi]:
                    for d in op.deps:
                        w = prog.waited[qi]
                        sem = d.sem if d.dma else prog.sems[d.qi]
                        key = id(sem)
                        if w.get(key, -1) >= d.val:
                            continue
                        w[key] = d.val
                        e.wait_ge(sem, d.val)
                    inst = op.fn(e)
                    if op.need:
                        if isinstance(inst, (list, tuple)):
                            inst = inst[-1]
                        if op.dma:
                            inst.then_inc(op.sem, 16)
                        else:
                            inst.then_inc(prog.sems[qi], 1)
                    op.flushed = True
                if prog.final_wait is not None and prog.final_wait.qi == qi \
                        and prog.final_wait in pend[qi]:
                    fw = prog.final_wait
                    e.wait_ge(fw.sem if fw.dma else prog.sems[qi], fw.val)
            return section

        for qi, name in enumerate(_QN):
            getattr(block, name)(mk(qi))


def _ap(t, off, dims):
    return bass.AP(t, off, [list(d) for d in dims])


def build_nc():
    nc = bass.Bass("TRN2", target_bir_lowering=False)

    xh_d = nc.dram_tensor("xh", [EXT, D], F32, kind="ExternalInput")
    xl_d = nc.dram_tensor("xl", [96, D], F32, kind="ExternalInput")
    wg_d = nc.dram_tensor("wg", [NK * D, D], F32, kind="ExternalInput")
    wv_d = nc.dram_tensor("wv", [D, D], F32, kind="ExternalInput")
    w1_d = nc.dram_tensor("w1", [D, H], F32, kind="ExternalInput")
    w2_d = nc.dram_tensor("w2", [H, D], F32, kind="ExternalInput")
    b1_d = nc.dram_tensor("b1", [H], F32, kind="ExternalInput")
    b2_d = nc.dram_tensor("b2", [D], F32, kind="ExternalInput")
    gam_d = nc.dram_tensor("gamma", [D], F32, kind="ExternalInput")
    bet_d = nc.dram_tensor("beta", [D], F32, kind="ExternalInput")
    scl_d = nc.dram_tensor("scal", [1], F32, kind="ExternalInput")
    out_d = nc.dram_tensor("out", [T_OWN, D], F32, kind="ExternalOutput")

    ikind = "ExternalOutput" if _DEBUG else "Internal"
    x1sp = nc.dram_tensor("x1sp", [T_OWN, D], F32, kind=ikind)
    y2ls_dram = nc.dram_tensor("y2ls_dram", [16, D], BF16, kind=ikind)
    if _DEBUG:
        dbg_y1t = nc.dram_tensor("dbg_y1t", [128, KO * EXT], BF16,
                                 kind="ExternalOutput")
        dbg_v = nc.dram_tensor("dbg_v", [128, KO * VW], F32,
                               kind="ExternalOutput")
        dbg_acc = nc.dram_tensor("dbg_acc", [128, KO * HALF], F32,
                                 kind="ExternalOutput")

    with contextlib.ExitStack() as outer:
        E = outer.enter_context
        sems = [E(nc.semaphore(f"s_{n}")) for n in _QN]
        dsems = [None] * 5
        for qi in (SC, GP, SY):
            dsems[qi] = [E(nc.semaphore(f"d_{_QN[qi]}_{l}"))
                         for l in range(NL)]
        P = _Prog(sems, dsems)

        id_f = E(nc.sbuf_tensor("id_f", [128, 128], F32))
        scale_b = E(nc.sbuf_tensor("scale_b", [128, 1], F32))
        b1c = E(nc.sbuf_tensor("b1c", [128, 32], F32))
        b2c = E(nc.sbuf_tensor("b2c", [128, 8], F32))
        odd1T = E(nc.sbuf_tensor("odd1T", [128, 32], BF16))
        O_all = E(nc.sbuf_tensor("O_all", [128, 512], BF16))
        M2 = E(nc.sbuf_tensor("M2", [16, D], BF16))
        stats = E(nc.sbuf_tensor("stats", [128, 64], F32))
        bnb = E(nc.sbuf_tensor("bnb", [128, 12], F32))
        cst = E(nc.sbuf_tensor("cst", [128, 4], F32))
        cst_tok = [None]

        slot_last = [None] * 8
        slot_rr = [0]

        def rstd(src_t, nrows, deps):
            s = slot_rr[0]
            slot_rr[0] = (s + 1) % 8
            st = stats
            c0 = s * 8
            dps = list(deps)
            a1 = P.emit(VE, lambda e, t=src_t, n=nrows: e.bn_stats(
                bnb[0:n, 0:6], t[0:n, 0:512]), deps=dps, tiny=True)
            a2 = P.emit(VE, lambda e, t=src_t, n=nrows: e.bn_stats(
                bnb[0:n, 6:12], t[0:n, 512:1024]), deps=dps + [a1], tiny=True)
            sdep = [slot_last[s]] if slot_last[s] is not None else []
            b = P.emit(VE, lambda e, n=nrows, c0=c0: e.bn_aggr(
                st[0:n, c0:c0 + 2], bnb[0:n, 0:12]),
                deps=[a1, a2] + sdep, tiny=True)
            a4 = P.emit(SC, lambda e, n=nrows, c0=c0: e.activation(
                st[0:n, c0 + 2:c0 + 3], st[0:n, c0 + 1:c0 + 2], AF.Sqrt,
                bias=cst[0:n, 0:1]), deps=[b, cst_tok[0]], tiny=True)
            a5 = P.emit(VE, lambda e, n=nrows, c0=c0: e.reciprocal(
                st[0:n, c0 + 3:c0 + 4], st[0:n, c0 + 2:c0 + 3]),
                deps=[a4], tiny=True)
            a6 = P.emit(VE, lambda e, n=nrows, c0=c0: e.tensor_scalar(
                st[0:n, c0 + 4:c0 + 5], st[0:n, c0:c0 + 1],
                st[0:n, c0 + 3:c0 + 4], cst[0:n, 1:2],
                AluOpType.mult, AluOpType.mult),
                deps=[a5, b, cst_tok[0]], tiny=True)
            return st[0:nrows, c0 + 3:c0 + 4], st[0:nrows, c0 + 4:c0 + 5], a6, s

        def slot_done(s, op):
            slot_last[s] = op

        cross = {}

        with contextlib.ExitStack() as attn:
            A = attn.enter_context
            y1T = A(nc.sbuf_tensor("y1T", [128, KO * EXT], BF16))
            y2L = A(nc.sbuf_tensor("y2L", [16, D], F32))
            y2LT = A(nc.sbuf_tensor("y2LT", [128, 128], BF16))
            y2ls_sb = A(nc.sbuf_tensor("y2ls_sb", [16, D], BF16))
            acc = A(nc.sbuf_tensor("acc", [128, KO * HALF], F32))
            v_all = A(nc.sbuf_tensor("v_all", [128, KO * VW], F32))
            wgt = A(nc.sbuf_tensor("wgt", [128, 2 * KO * D], BF16))
            wvt = A(nc.sbuf_tensor("wvt", [128, KO * D], BF16))
            x0r = [A(nc.sbuf_tensor(f"x0r{i}", [128, D], F32)) for i in range(2)]
            g_buf = [A(nc.sbuf_tensor(f"gb{i}", [128, D], F32)) for i in range(2)]
            wring = [A(nc.sbuf_tensor(f"wr{i}", [128, D], F32)) for i in range(4)]
            vL = A(nc.sbuf_tensor("vL", [128, KO * 88], F32))
            accL = A(nc.sbuf_tensor("accL", [128, KO * 16], F32))
            y1LT = A(nc.sbuf_tensor("y1LT", [128, KO * 128], BF16))
            gl_sb = A(nc.sbuf_tensor("gl_sb", [128, 16], F32))
            gam_b = A(nc.sbuf_tensor("gam_b", [128, D], F32))
            bet_b = A(nc.sbuf_tensor("bet_b", [128, D], F32))
            xlt = A(nc.sbuf_tensor("xlt", [128, D], F32))
            w_z = g_buf[0]
            lwB = g_buf[1]

            p_va = A(nc.psum_tensor("p_va", [128, 512], F32))
            p_vb = A(nc.psum_tensor("p_vb", [128, 512], F32))
            p_g = [A(nc.psum_tensor(f"p_g{i}", [128, 1024], F32)) for i in range(2)]
            p_gl = A(nc.psum_tensor("p_gl", [128, 512], F32))
            p_tr = A(nc.psum_tensor("p_tr", [128, 512], F32))

            ce = P.emit(GP, lambda e: e.memset(cst[:, 0:1], EPS))
            ce2 = P.emit(GP, lambda e: e.memset(cst[:, 1:2], -1.0), deps=[ce])
            ce3 = P.emit(GP, lambda e: e.memset(cst[:, 2:3], 0.5), deps=[ce2])
            cst_tok[0] = P.emit(GP, lambda e: e.memset(cst[:, 3:4], 0.25),
                                deps=[ce3])
            idm = P.emit(GP, lambda e: e.memset(id_f[:, :], 0.0))
            ids = P.emit(GP, lambda e: e.affine_select(
                out=id_f[:, :], in_=id_f[:, :], compare_op=AluOpType.not_equal,
                fill=1.0, base=0, pattern=[[-1, 128]], channel_multiplier=1),
                deps=[idm])

            ld_sc = P.emit(SY, lambda e: e.dma_start(
                scale_b[:, :], _ap(scl_d, 0, [[0, 128], [1, 1]])), dma=True)
            ld_g = P.emit(SY, lambda e: e.dma_start(
                gam_b[:, :], _ap(gam_d, 0, [[0, 128], [1, D]])), dma=True)
            ld_b = P.emit(SY, lambda e: e.dma_start(
                bet_b[:, :], _ap(bet_d, 0, [[0, 128], [1, D]])), dma=True)
            ld_b1 = P.emit(SY, lambda e: e.dma_start(
                xlt[0:32, 0:128], _ap(b1_d, 0, [[128, 32], [1, 128]])), dma=True)
            ld_b2 = P.emit(SY, lambda e: e.dma_start(
                xlt[0:8, 256:384], _ap(b2_d, 0, [[128, 8], [1, 128]])), dma=True)

            t_b1 = P.emit(TE, lambda e: e.transpose(
                p_tr[:, 0:32], xlt[0:32, 0:128], id_f[0:32, 0:32]),
                deps=[ld_b1, ids])
            e_b1 = P.emit(VE, lambda e: e.tensor_copy(b1c[:, :], p_tr[:, 0:32]),
                          deps=[t_b1], need=True)
            t_b2 = P.emit(TE, lambda e: e.transpose(
                p_tr[:, 32:40], xlt[0:8, 256:384], id_f[0:8, 0:8]),
                deps=[ld_b2, e_b1])
            e_b2 = P.emit(VE, lambda e: e.tensor_copy(b2c[:, :], p_tr[:, 32:40]),
                          deps=[t_b2], need=True)
            ptr_free = e_b2

            wvt_ops = []
            for ko in range(KO):
                wvt_ops.append(P.emit(GP, lambda e, ko=ko: e.dma_start(
                    wvt[:, ko * D:(ko + 1) * D],
                    wv_d.ap()[ko * 128:(ko + 1) * 128, :]),
                    dma=True, need=True))
            wvt_done = wvt_ops[-1]

            mz = P.emit(GP, lambda e: e.memset(xlt[:, :], 0.0),
                        deps=[t_b1, t_b2])
            ld_xl = P.emit(SY, lambda e: e.dma_start(xlt[0:96, :], xl_d.ap()[:, :]),
                           dma=True, deps=[mz])
            r_ap, m_ap, rop, sl = rstd(xlt, 128, [ld_xl])
            nrm_xl = P.emit(SC, lambda e, r=r_ap, m=m_ap: e.activation(
                wring[2][:, :], xlt[:, :], AF.Identity, bias=m, scale=r),
                deps=[rop])
            slot_done(sl, nrm_xl)
            tl_last = None
            for hb in range(2):
                tro = []
                for j in range(4):
                    ko = hb * 4 + j
                    tro.append(P.emit(TE, lambda e, ko=ko, j=j: e.transpose(
                        p_tr[:, j * 128:(j + 1) * 128],
                        wring[2][:, ko * 128:(ko + 1) * 128], id_f[:, :]),
                        deps=[nrm_xl, ids, ptr_free]))
                ev = P.emit(VE, lambda e, hb=hb: e.tensor_copy(
                    _ap(y1LT, hb * 4 * 128, [[KO * 128, 128], [128, 4], [1, 128]]),
                    _ap(p_tr, 0, [[512, 128], [128, 4], [1, 128]])),
                    deps=[tro[-1]], need=True)
                ptr_free = ev
                tl_last = ev
            y1LT_done = tl_last
            wring_free = [None, None, tl_last, None]

            x0r_free = [None, None]
            y1t_ev = [None] * 17

            def p1_tile(i):
                nonlocal ptr_free
                sl_i = i % 2
                wsl = i % 4
                ld = P.emit(SY, lambda e, i=i, s=sl_i: e.dma_start(
                    x0r[s][:, :], xh_d.ap()[i * 128:(i + 1) * 128, :]),
                    dma=True, deps=[x0r_free[sl_i]])
                r_ap, m_ap, rop, sl = rstd(x0r[sl_i], 128, [ld])
                nrm = P.emit(SC, lambda e, s=sl_i, w=wsl, r=r_ap, m=m_ap:
                             e.activation(wring[w][:, :], x0r[s][:, :],
                                          AF.Identity, bias=m, scale=r),
                             deps=[rop, wring_free[wsl]])
                slot_done(sl, nrm)
                x0r_free[sl_i] = nrm
                last_ev = None
                for hb in range(2):
                    tro = []
                    for j in range(4):
                        ko = hb * 4 + j
                        tro.append(P.emit(TE, lambda e, w=wsl, ko=ko, j=j:
                                          e.transpose(
                            p_tr[:, j * 128:(j + 1) * 128],
                            wring[w][:, ko * 128:(ko + 1) * 128], id_f[:, :]),
                            deps=[nrm, ptr_free]))
                    ev = P.emit(VE, lambda e, hb=hb, i=i: e.tensor_copy(
                        _ap(y1T, hb * 4 * EXT + i * 128,
                            [[KO * EXT, 128], [EXT, 4], [1, 128]]),
                        _ap(p_tr, 0, [[512, 128], [128, 4], [1, 128]])),
                        deps=[tro[-1]], need=True)
                    ptr_free = ev
                    last_ev = ev
                wring_free[wsl] = last_ev
                y1t_ev[i] = last_ev

            for i in range(9):
                p1_tile(i)

            def xbar_wg(k, dep_free):
                pk = k % 2
                ops = []
                for ko in range(KO):
                    ops.append(P.emit(GP, lambda e, k=k, ko=ko, pk=pk: e.dma_start(
                        wgt[:, pk * KO * D + ko * D:pk * KO * D + (ko + 1) * D],
                        wg_d.ap()[k * D + ko * 128:k * D + (ko + 1) * 128, :]),
                        dma=True, deps=[dep_free], need=True))
                return ops[-1]

            acc_free = None
            accL_done = None
            tt_last = None
            v_evac = [None] * KO
            vL_evac = [None] * KO
            gbuf_free = [None, None]
            sig_last = [None, None]
            pva_free, pvb_free = None, None
            wg_free = [None, None]
            x1sp_ops = []
            o_last = od_last = st_ls = None
            m2g = []

            def v_emit(h, tt_gate):
                nonlocal pva_free, pvb_free
                chunks = [(0, 512), (512, 512), (1024, 64)]
                v_par = 0
                for dg in range(KO):
                    my_evac = None
                    for (coff, cw) in chunks:
                        ps = p_va if v_par == 0 else p_vb
                        psfree = pva_free if v_par == 0 else pvb_free
                        mms = []
                        for ko in range(KO):
                            mms.append(P.emit(TE, lambda e, ps=ps, ko=ko, dg=dg,
                                              h=h, coff=coff, cw=cw: e.matmul(
                                ps[:, 0:cw],
                                wvt[:, ko * D + dg * 128:ko * D + (dg + 1) * 128],
                                y1T[:, ko * EXT + 32 + h * HALF + coff:
                                    ko * EXT + 32 + h * HALF + coff + cw],
                                start=(ko == 0), stop=(ko == KO - 1)),
                                deps=([wvt_done, psfree, y1t_ev[min(
                                    16, (32 + h * HALF + coff + cw - 1) // 128)]]
                                      if ko == 0 else [])))
                        ev = P.emit(SC, lambda e, ps=ps, dg=dg, coff=coff, cw=cw:
                                    e.copy(v_all[:, dg * VW + coff:
                                                 dg * VW + coff + cw],
                                           ps[:, 0:cw]),
                                    deps=[mms[-1], tt_gate], need=True)
                        my_evac = ev
                        if v_par == 0:
                            pva_free = ev
                        else:
                            pvb_free = ev
                        v_par ^= 1
                    v_evac[dg] = my_evac
                    if h == 0:
                        ps = p_va if v_par == 0 else p_vb
                        psfree = pva_free if v_par == 0 else pvb_free
                        mms = []
                        for ko in range(KO):
                            mms.append(P.emit(TE, lambda e, ps=ps, ko=ko, dg=dg:
                                              e.matmul(
                                ps[:, 0:96],
                                wvt[:, ko * D + dg * 128:ko * D + (dg + 1) * 128],
                                y1LT[:, ko * 128:ko * 128 + 96],
                                start=(ko == 0), stop=(ko == KO - 1)),
                                deps=([y1LT_done, psfree] if ko == 0 else [])))
                        ev = P.emit(SC, lambda e, ps=ps, dg=dg: e.copy(
                            vL[:, dg * 88:(dg + 1) * 88], ps[:, 1:89]),
                            deps=[mms[-1]], need=True)
                        vL_evac[dg] = ev
                        if v_par == 0:
                            pva_free = ev
                        else:
                            pvb_free = ev
                        v_par ^= 1

            for h in range(2):
                if h == 0:
                    v_emit(0, None)
                    for i in range(9, 17):
                        p1_tile(i)

                xb = [xbar_wg(0, wg_free[0]), xbar_wg(1, wg_free[1])]
                for k in range(NK):
                    pk = k % 2
                    s_k = SHIFTS[k]
                    for dg in range(KO):
                        pg = dg % 2
                        mms = []
                        for ko in range(KO):
                            for nch in range(2):
                                mms.append(P.emit(TE, lambda e, pg=pg, pk=pk,
                                                  ko=ko, dg=dg, h=h, nch=nch:
                                                  e.matmul(
                                    p_g[pg][:, nch * 512:(nch + 1) * 512],
                                    wgt[:, pk * KO * D + ko * D + dg * 128:
                                        pk * KO * D + ko * D + (dg + 1) * 128],
                                    y1T[:, ko * EXT + 64 + h * HALF + nch * 512:
                                        ko * EXT + 64 + h * HALF +
                                        (nch + 1) * 512],
                                    start=(ko == 0), stop=(ko == KO - 1)),
                                    deps=([xb[pk], sig_last[pg],
                                           y1t_ev[8 if h == 0 else 16]]
                                          if (ko == 0 and nch == 0) else [])))
                        wg_free[pk] = mms[-1]
                        sig = P.emit(SC, lambda e, pg=pg: e.activation(
                            g_buf[pg][:, :], p_g[pg][:, :], AF.Sigmoid),
                            deps=[mms[-1], gbuf_free[pg]], need=True)
                        sig_last[pg] = sig
                        if k == 0:
                            t2 = P.emit(VE, lambda e, pg=pg, dg=dg, s=s_k:
                                        e.tensor_tensor(
                                acc[:, dg * HALF:(dg + 1) * HALF],
                                g_buf[pg][:, :],
                                v_all[:, dg * VW + 32 + s:
                                       dg * VW + 32 + s + HALF],
                                AluOpType.mult),
                                deps=[sig, v_evac[dg], acc_free])
                        else:
                            t1 = P.emit(VE, lambda e, pg=pg, dg=dg, s=s_k:
                                        e.tensor_tensor(
                                g_buf[pg][:, :], g_buf[pg][:, :],
                                v_all[:, dg * VW + 32 + s:
                                       dg * VW + 32 + s + HALF],
                                AluOpType.mult),
                                deps=[sig, v_evac[dg]])
                            t2 = P.emit(VE, lambda e, pg=pg, dg=dg:
                                        e.tensor_tensor(
                                acc[:, dg * HALF:(dg + 1) * HALF],
                                acc[:, dg * HALF:(dg + 1) * HALF],
                                g_buf[pg][:, :], AluOpType.add), deps=[t1])
                        tt_last = t2
                        gbuf_free[pg] = t2
                        if h == 0:
                            lm = []
                            for ko in range(KO):
                                lm.append(P.emit(TE, lambda e, pk=pk, ko=ko,
                                                 dg=dg: e.matmul(
                                    p_gl[:, 0:16],
                                    wgt[:, pk * KO * D + ko * D + dg * 128:
                                        pk * KO * D + ko * D + (dg + 1) * 128],
                                    y1LT[:, ko * 128 + 33:ko * 128 + 49],
                                    start=(ko == 0), stop=(ko == KO - 1)),
                                    deps=([accL_done, y1LT_done]
                                          if ko == 0 else [])))
                            sigL = P.emit(SC, lambda e: e.activation(
                                gl_sb[:, :], p_gl[:, 0:16], AF.Sigmoid),
                                deps=[lm[-1], accL_done], need=True)
                            pgl_free = sigL
                            if k == 0:
                                tL = P.emit(VE, lambda e, dg=dg, s=s_k:
                                            e.tensor_tensor(
                                    accL[:, dg * 16:(dg + 1) * 16], gl_sb[:, :],
                                    vL[:, dg * 88 + 32 + s:dg * 88 + 48 + s],
                                    AluOpType.mult),
                                    deps=[sigL, vL_evac[dg]], tiny=True)
                            else:
                                tLa = P.emit(VE, lambda e, dg=dg, s=s_k:
                                             e.tensor_tensor(
                                    gl_sb[:, :], gl_sb[:, :],
                                    vL[:, dg * 88 + 32 + s:dg * 88 + 48 + s],
                                    AluOpType.mult),
                                    deps=[sigL, vL_evac[dg]], tiny=True)
                                tL = P.emit(VE, lambda e, dg=dg: e.tensor_tensor(
                                    accL[:, dg * 16:(dg + 1) * 16],
                                    accL[:, dg * 16:(dg + 1) * 16],
                                    gl_sb[:, :], AluOpType.add),
                                    deps=[tLa, accL_done], tiny=True)
                            accL_done = tL
                    if k + 2 < NK:
                        xb[(k + 2) % 2] = xbar_wg(k + 2, wg_free[(k + 2) % 2])

                if h == 0:
                    evs = []
                    for hb in range(2):
                        tro = []
                        for j in range(4):
                            dg = hb * 4 + j
                            tro.append(P.emit(TE, lambda e, dg=dg, j=j:
                                              e.transpose(
                                p_tr[0:16, j * 128:(j + 1) * 128],
                                accL[:, dg * 16:(dg + 1) * 16], id_f[:, :]),
                                deps=[accL_done, ptr_free]))
                        ev = P.emit(VE, lambda e, hb=hb: e.tensor_copy(
                            _ap(xlt, hb * 512, [[D, 16], [128, 4], [1, 128]]),
                            _ap(p_tr, 0, [[512, 16], [128, 4], [1, 128]])),
                            deps=[tro[-1]], need=True, tiny=True)
                        ptr_free = ev
                        evs.append(ev)
                    accLnat = evs[-1]
                    ld_x0L = P.emit(SY, lambda e: e.dma_start(
                        x0r[0][0:16, :], xl_d.ap()[33:49, :]), dma=True,
                        deps=[x0r_free[0]])
                    r1, n1, rop1, sl1 = rstd(xlt, 16, [accLnat])
                    nL = P.emit(SC, lambda e, r=r1, m=n1: e.activation(
                        lwB[0:16, :], xlt[0:16, :], AF.Identity, bias=m, scale=r),
                        deps=[rop1, accLnat, gbuf_free[1]])
                    slot_done(sl1, nL)
                    P.emit(VE, lambda e: e.tensor_tensor(
                        lwB[0:16, :], lwB[0:16, :], gam_b[0:16, :],
                        AluOpType.mult), deps=[nL, ld_g])
                    zL2 = P.emit(VE, lambda e: e.tensor_tensor(
                        lwB[0:16, :], lwB[0:16, :], bet_b[0:16, :],
                        AluOpType.add), deps=[ld_b])
                    r2, n2, rop2, sl2 = rstd(lwB, 16, [zL2])
                    n2L = P.emit(SC, lambda e, r=r2, m=n2: e.activation(
                        xlt[0:16, :], lwB[0:16, :], AF.Identity, bias=m, scale=r),
                        deps=[rop2])
                    slot_done(sl2, n2L)
                    gbuf_free[1] = n2L
                    x1L = P.emit(VE, lambda e: e.tensor_tensor(
                        xlt[0:16, :], xlt[0:16, :], x0r[0][0:16, :],
                        AluOpType.add),
                        deps=[n2L, ld_x0L])
                    x0r_free[0] = x1L
                    r3, n3, rop3, sl3 = rstd(xlt, 16, [x1L])
                    y2L_op = P.emit(SC, lambda e, r=r3, m=n3: e.activation(
                        y2L[:, :], xlt[0:16, :], AF.Identity, bias=m, scale=r),
                        deps=[rop3, x1L])
                    slot_done(sl3, y2L_op)
                    tro = []
                    for ko in range(KO):
                        tro.append(P.emit(TE, lambda e, ko=ko: e.transpose(
                            p_tr[:, ko * 16:(ko + 1) * 16],
                            y2L[:, ko * 128:(ko + 1) * 128], id_f[0:16, 0:16]),
                            deps=[y2L_op, ptr_free]))
                    y2LT_op = P.emit(VE, lambda e: e.tensor_copy(
                        y2LT[:, :], p_tr[:, 0:128]), deps=[tro[-1]], need=True,
                        tiny=True)
                    ptr_free = y2LT_op
                    y2ls_op = P.emit(SC, lambda e: e.activation(
                        y2ls_sb[:, :], y2L[:, :], AF.Identity,
                        scale=scale_b[0:16, 0:1]), deps=[y2L_op, ld_sc])
                    st_ls = P.emit(SY, lambda e: e.dma_start(
                        y2ls_dram.ap(), y2ls_sb[:, :]), dma=True,
                        deps=[y2ls_op], need=True)
                    m2z = P.emit(GP, lambda e: e.memset(M2[:, :], 0.0))
                    for j in range(4):
                        for pm in range(4):
                            m2g.append(P.emit(GP, lambda e, j=j, pm=pm:
                                              e.dma_start(
                                M2[j * 4 + pm:j * 4 + pm + 1,
                                   j * 256:(j + 1) * 256],
                                _ap(y2ls_dram, (8 + 2 * j) * D + pm,
                                    [[1, 1], [4, 256]]),
                                allow_slow_non_contiguous=True),
                                dma=True, deps=[st_ls, m2z], need=True))
                    oz = P.emit(VE, lambda e: e.memset(O_all[:, :], 0.0),
                                deps=[y2LT_op], tiny=True)
                    for g in range(32):
                        pm, ko = g // 8, g % 8
                        o_last = P.emit(VE, lambda e, g=g, pm=pm, ko=ko:
                                        e.tensor_copy(
                            _ap(O_all, g * 16 + pm, [[512, 128], [4, 4]]),
                            _ap(y2LT, ko * 16 + 9, [[128, 128], [2, 4]])),
                            deps=[oz, y2LT_op], need=(g == 31))
                    for ko in range(KO):
                        od_last = P.emit(VE, lambda e, ko=ko: e.tensor_copy(
                            _ap(odd1T, ko * 4, [[32, 128], [1, 4]]),
                            _ap(y2LT, ko * 16 + 1, [[128, 128], [2, 4]])),
                            deps=[y2LT_op], need=(ko == KO - 1))
                    v_emit(1, tt_last)

                tt_final = tt_last
                tr_last = None
                wz_free = [gbuf_free[0], gbuf_free[1]]
                if h == 0:
                    ering = wring + x0r
                    ering_free = [wring_free[0], wring_free[1], wring_free[2],
                                  wring_free[3], x0r_free[0], x0r_free[1]]
                for i in range(8):
                    wsl = i % 6
                    tro0 = []
                    for j in range(4):
                        tro0.append(P.emit(TE, lambda e, i=i, j=j:
                                           e.transpose(
                            p_tr[:, j * 128:(j + 1) * 128],
                            acc[:, j * HALF + i * 128:
                                j * HALF + (i + 1) * 128],
                            id_f[:, :]),
                            deps=[tt_final, ptr_free]))
                    ev0 = P.emit(VE, lambda e, w=wsl: e.tensor_copy(
                        ering[w][:, 0:512], p_tr[:, 0:512]),
                        deps=[tro0[-1], ering_free[wsl]], need=True, tiny=True)
                    ptr_free = ev0
                    tro1 = []
                    for j in range(4):
                        dg = 4 + j
                        tro1.append(P.emit(TE, lambda e, dg=dg, i=i, j=j:
                                           e.transpose(
                            p_gl[:, j * 128:(j + 1) * 128],
                            acc[:, dg * HALF + i * 128:
                                dg * HALF + (i + 1) * 128],
                            id_f[:, :]),
                            deps=[tt_final, pgl_free]))
                    tr_last = tro1[-1]
                    ev1 = P.emit(SC, lambda e, w=wsl: e.copy(
                        ering[w][:, 512:1024], p_gl[:, 0:512]),
                        deps=[tro1[-1], ering_free[wsl]], need=True, tiny=True)
                    pgl_free = ev1
                    natacc = ev1
                    wz = g_buf[i % 2]
                    r1, n1, rop1, sl1 = rstd(ering[wsl], 128, [ev0, ev1])
                    nrm1 = P.emit(SC, lambda e, wz=wz, w=wsl, r=r1, m=n1:
                                  e.activation(
                        wz[:, :], ering[w][:, :], AF.Identity, bias=m, scale=r),
                        deps=[rop1, ev0, ev1, wz_free[i % 2]])
                    slot_done(sl1, nrm1)
                    P.emit(VE, lambda e, wz=wz: e.tensor_tensor(
                        wz[:, :], wz[:, :], gam_b[:, :], AluOpType.mult),
                        deps=[nrm1, ld_g])
                    z2 = P.emit(VE, lambda e, wz=wz: e.tensor_tensor(
                        wz[:, :], wz[:, :], bet_b[:, :], AluOpType.add),
                        deps=[ld_b])
                    r2, n2, rop2, sl2 = rstd(wz, 128, [z2])
                    nrm2 = P.emit(SC, lambda e, wz=wz, w=wsl, r=r2, m=n2:
                                  e.activation(
                        ering[w][:, :], wz[:, :], AF.Identity, bias=m, scale=r),
                        deps=[rop2])
                    slot_done(sl2, nrm2)
                    wz_free[i % 2] = nrm2
                    sp = P.emit(SY, lambda e, h=h, i=i, w=wsl: e.dma_start(
                        x1sp.ap()[h * HALF + i * 128:h * HALF + (i + 1) * 128, :],
                        ering[w][:, :]), dma=True, deps=[nrm2], need=True)
                    x1sp_ops.append(sp)
                    ering_free[wsl] = sp
                acc_free = tr_last
                gbuf_free[0] = wz_free[0]
                gbuf_free[1] = wz_free[1]

            spill_done = x1sp_ops[-1]
            if _DEBUG:
                P.emit(SY, lambda e: e.dma_start(dbg_y1t.ap(), y1T[:, :]),
                       dma=True, deps=[y1t_ev[16]], need=True)
                P.emit(SY, lambda e: e.dma_start(dbg_v.ap(), v_all[:, :]),
                       dma=True, deps=[v_evac[KO - 1]], need=True)
                P.emit(SY, lambda e: e.dma_start(dbg_acc.ap(), acc[:, :]),
                       dma=True, deps=[tt_last], need=True)
            bar = P.barrier()
            cross.update(m2g=m2g,
                         o_last=o_last, od_last=od_last, st_ls=st_ls,
                         e_b1=e_b1, e_b2=e_b2, spill_done=spill_done, bar=bar,
                         x1sp_ops=x1sp_ops)

            blk1 = attn.enter_context(nc.Block())
            P.flush_into(blk1)

        for s in range(8):
            slot_last[s] = None

        with contextlib.ExitStack() as ffn:
            B = ffn.enter_context
            w1t = B(nc.sbuf_tensor("w1t", [128, KO * H], BF16))
            w2t = B(nc.sbuf_tensor("w2t", [128, MG * D], BF16))
            h_bf = B(nc.sbuf_tensor("h_bf", [128, MG * FT], BF16))
            y2t = [B(nc.sbuf_tensor(f"y2t{i}", [128, KO * FT], BF16))
                   for i in range(2)]
            x1r = [B(nc.sbuf_tensor(f"x1r{i}", [128, D], F32)) for i in range(4)]
            wf = [B(nc.sbuf_tensor(f"wf{i}", [128, D], F32)) for i in range(4)]
            w_odt = B(nc.sbuf_tensor("w_odt", [128, KO * FT], F32))
            M1 = B(nc.sbuf_tensor("M1", [4, H], BF16))
            c_bf = B(nc.sbuf_tensor("c_bf", [4, FT], BF16))
            t_bf = B(nc.sbuf_tensor("t_bf", [16, FT], BF16))

            p_c = B(nc.psum_tensor("p_c", [4, 512], F32))
            p_t = B(nc.psum_tensor("p_t", [16, 512], F32))
            p_h = [B(nc.psum_tensor(f"p_h{i}", [128, 512], F32))
                   for i in range(2)]
            p_o = [B(nc.psum_tensor(f"p_o{i}", [128, 512], F32))
                   for i in range(2)]
            p_f = B(nc.psum_tensor("p_f", [128, 1024], F32))

            bar = cross["bar"]
            bsy, bgp = bar[SY], bar[GP]
            w1t_ops = []
            for g in range(8):
                w1t_ops.append(P.emit(GP, lambda e, g=g: e.dma_start(
                    _ap(w1t, g * 512, [[KO * H, 128], [H, KO], [1, 512]]),
                    _ap(w1_d, g * 512, [[H, 128], [128 * H, KO], [1, 512]])),
                    dma=True, deps=[bgp], need=True))
            w2t_ops = []
            for ko in range(MG):
                w2t_ops.append(P.emit(GP, lambda e, ko=ko: e.dma_start(
                    w2t[:, ko * D:(ko + 1) * D],
                    w2_d.ap()[ko * 128:(ko + 1) * 128, :]),
                    dma=True, deps=[bgp], need=True))
            m1z = P.emit(GP, lambda e: e.memset(M1[:, :], 0.0), deps=[bgp])
            m1_done = None
            for j in range(4):
                m1_done = P.emit(SY, lambda e, j=j: e.dma_start(
                    M1[j:j + 1, j * D:(j + 1) * D],
                    y2ls_dram.ap()[2 * j:2 * j + 1, :]),
                    dma=True, deps=[m1z, cross["st_ls"], bsy], need=True)

            ptr2_free = bar[VE]
            x1r_free = [None] * 4
            wf_free = [None] * 4
            y2t_free = [None, None]
            hbf_free = None
            codt_free = None
            gel_prev = [None, None]
            oev_prev = [None, None]
            prev_cev = None
            prev_tev = None
            cbf_free = None
            tbf_free = None
            out_ops = []

            def pro(i):
                sub_nrm = []
                adds = []
                for sub in range(2):
                    s = (2 * i + sub) % 4
                    wslot = (2 * i + sub) % 4
                    lda = P.emit(SY, lambda e, i=i, sub=sub, s=s: e.dma_start(
                        x1r[s][:, :],
                        x1sp.ap()[i * FT + sub * 128:i * FT + sub * 128 + 128, :]),
                        dma=True, deps=[x1r_free[s],
                                        cross["x1sp_ops"][2 * i + sub]])
                    ldx = P.emit(SY, lambda e, i=i, sub=sub, w=wslot:
                                 e.dma_start(
                        wf[w][:, :],
                        xh_d.ap()[64 + i * FT + sub * 128:
                                  64 + i * FT + sub * 128 + 128, :]),
                        dma=True, deps=[wf_free[wslot]])
                    xadd = P.emit(VE, lambda e, s=s, w=wslot: e.tensor_tensor(
                        x1r[s][:, :], x1r[s][:, :], wf[w][:, :],
                        AluOpType.add), deps=[lda, ldx])
                    adds.append(xadd)
                    r_ap, m_ap, rop, sl = rstd(x1r[s], 128, [xadd])
                    nrm = P.emit(SC, lambda e, s=s, w=wslot, r=r_ap, m=m_ap:
                                 e.activation(wf[w][:, :], x1r[s][:, :],
                                              AF.Identity, bias=m, scale=r),
                                 deps=[rop, xadd])
                    slot_done(sl, nrm)
                    sub_nrm.append((nrm, wslot))
                return {"lds": adds, "sub_nrm": sub_nrm}

            pro_d = [None] * NFT
            pro_d[0] = pro(0)
            for i in range(NFT):
                yp = i % 2
                lds = pro_d[i]["lds"]
                tr_ev = []
                for sub in range(2):
                    nrm, wslot = pro_d[i]["sub_nrm"][sub]
                    tro = []
                    for ko in range(KO):
                        tro.append(P.emit(TE, lambda e, w=wslot, ko=ko:
                                          e.transpose(
                            p_f[:, ko * 128:(ko + 1) * 128],
                            wf[w][:, ko * 128:(ko + 1) * 128], id_f[:, :]),
                            deps=[nrm, ptr2_free, y2t_free[yp]]))
                    ev = P.emit(SC, lambda e, yp=yp, sub=sub: e.copy(
                        _ap(y2t[yp], sub * 128,
                            [[KO * FT, 128], [FT, KO], [1, 128]]),
                        _ap(p_f, 0, [[1024, 128], [128, KO], [1, 128]])),
                        deps=[tro[-1]], need=True)
                    ptr2_free = ev
                    tr_ev.append(ev)
                    wf_free[wslot] = ev
                y2t_done = tr_ev[-1]
                cms = []
                for ko in range(KO):
                    cms.append(P.emit(TE, lambda e, yp=yp, ko=ko: e.matmul(
                        p_c[:, 0:FT], odd1T[:, ko * 4:(ko + 1) * 4],
                        y2t[yp][:, ko * FT:(ko + 1) * FT],
                        start=(ko == 0), stop=(ko == KO - 1)),
                        deps=([y2t_done, cross["od_last"], prev_cev]
                              if ko == 0 else [])))
                cev = P.emit(VE, lambda e: e.tensor_copy(c_bf[:, :],
                                                         p_c[:, 0:FT]),
                             deps=[cms[-1], cbf_free], need=True)
                prev_cev = cev
                gel_last = None
                hms = []
                for mg in range(MG):
                    hp = mg % 2
                    hms = []
                    for ko in range(KO):
                        hms.append(P.emit(TE, lambda e, hp=hp, yp=yp, ko=ko,
                                          mg=mg: e.matmul(
                            p_h[hp][:, 0:FT],
                            w1t[:, ko * H + mg * 128:ko * H + (mg + 1) * 128],
                            y2t[yp][:, ko * FT:(ko + 1) * FT],
                            start=(ko == 0), stop=False),
                            deps=([w1t_ops[mg // 4], y2t_done, gel_prev[hp]]
                                  if ko == 0 else [])))
                    hms.append(P.emit(TE, lambda e, hp=hp, mg=mg: e.matmul(
                        p_h[hp][:, 0:FT], M1[:, mg * 128:(mg + 1) * 128],
                        c_bf[:, :], start=False, stop=True),
                        deps=[cev, m1_done]))
                    gel = P.emit(SC, lambda e, hp=hp, mg=mg: e.activation(
                        h_bf[:, mg * FT:(mg + 1) * FT], p_h[hp][:, 0:FT],
                        AF.Gelu, bias=b1c[:, mg:mg + 1]),
                        deps=[hms[-1], cross["e_b1"], hbf_free], need=True)
                    gel_prev[hp] = gel
                    gel_last = gel
                cbf_free = hms[-1]
                if i + 1 < NFT:
                    pro_d[i + 1] = pro(i + 1)
                tms = []
                for g in range(MG):
                    tms.append(P.emit(TE, lambda e, g=g: e.matmul(
                        p_t[:, 0:FT], O_all[:, g * 16:(g + 1) * 16],
                        h_bf[:, g * FT:(g + 1) * FT],
                        start=(g == 0), stop=(g == MG - 1)),
                        deps=([gel_last, cross["o_last"], prev_tev]
                              if g == 0 else [])))
                tev = P.emit(VE, lambda e: e.tensor_copy(t_bf[:, :],
                                                         p_t[:, 0:FT]),
                             deps=[tms[-1], tbf_free], need=True)
                prev_tev = tev
                oev_last = None
                oms = []
                for dg in range(KO):
                    op_ = dg % 2
                    oms = []
                    for ko in range(MG):
                        oms.append(P.emit(TE, lambda e, op_=op_, ko=ko, dg=dg:
                                          e.matmul(
                            p_o[op_][:, 0:FT],
                            w2t[:, ko * D + dg * 128:ko * D + (dg + 1) * 128],
                            h_bf[:, ko * FT:(ko + 1) * FT],
                            start=(ko == 0), stop=False),
                            deps=([w2t_ops[ko], gel_last, oev_prev[op_]]
                                  if ko == 0 else [w2t_ops[ko]])))
                    oms.append(P.emit(TE, lambda e, op_=op_, dg=dg: e.matmul(
                        p_o[op_][:, 0:FT], M2[:, dg * 128:(dg + 1) * 128],
                        t_bf[:, :], start=False, stop=True),
                        deps=[tev] + cross["m2g"]))
                    oev = P.emit(SC, lambda e, op_=op_, dg=dg: e.activation(
                        w_odt[:, dg * FT:(dg + 1) * FT], p_o[op_][:, 0:FT],
                        AF.Identity, bias=b2c[:, dg:dg + 1]),
                        deps=[oms[-1], cross["e_b2"], codt_free], need=True)
                    oev_prev[op_] = oev
                    oev_last = oev
                hbf_free = oms[-1]
                tbf_free = oms[-1]
                y2t_free[yp] = hms[-1]
                tro = []
                for sub in range(2):
                    tro = []
                    for dg in range(KO):
                        tro.append(P.emit(TE, lambda e, dg=dg, sub=sub:
                                          e.transpose(
                            p_f[:, dg * 128:(dg + 1) * 128],
                            w_odt[:, dg * FT + sub * 128:
                                  dg * FT + sub * 128 + 128],
                            id_f[:, :]),
                            deps=[oev_last, ptr2_free]))
                    wslot = (2 * i + sub) % 4
                    ev = P.emit(VE, lambda e, w=wslot: e.tensor_copy(
                        wf[w][:, :], p_f[:, :]),
                        deps=[tro[-1], wf_free[wslot]], need=True, tiny=True)
                    ptr2_free = ev
                    r_ap, m_ap, rop, sl = rstd(wf[wslot], 128, [ev])
                    nrm = P.emit(SC, lambda e, w=wslot, r=r_ap, m=m_ap:
                                 e.activation(wf[w][:, :], wf[w][:, :],
                                              AF.Identity, bias=m, scale=r),
                                 deps=[rop, ev])
                    slot_done(sl, nrm)
                    xsl = (2 * i + sub) % 4
                    res = P.emit(VE, lambda e, w=wslot, xs=xsl: e.tensor_tensor(
                        wf[w][:, :], wf[w][:, :], x1r[xs][:, :], AluOpType.add),
                        deps=[nrm, lds[sub]])
                    st = P.emit(SY, lambda e, i=i, sub=sub, w=wslot:
                                e.dma_start(
                        out_d.ap()[i * FT + sub * 128:
                                   i * FT + sub * 128 + 128, :],
                        wf[w][:, :]), dma=True, deps=[res], need=True)
                    out_ops.append(st)
                    wf_free[wslot] = st
                    x1r_free[xsl] = res
                codt_free = tro[-1]

            P.final_wait = out_ops[-1]
            blk2 = ffn.enter_context(nc.Block())
            P.flush_into(blk2)
    return nc


def kernel(**inputs):
    x = np.asarray(inputs["L0_attn_x"], dtype=np.float32)
    wg0 = np.asarray(inputs["L0_attn_gate_W"], np.float32)
    wg = np.ascontiguousarray(
        wg0.reshape(NK, D, D).transpose(0, 2, 1)).reshape(NK * D, D)
    wv = np.ascontiguousarray(np.asarray(inputs["L0_attn_value_W"], np.float32).T)
    gam = np.ascontiguousarray(np.asarray(inputs["L0_attn_gamma"], np.float32))
    bet = np.ascontiguousarray(np.asarray(inputs["L0_attn_beta"], np.float32))
    w1 = np.ascontiguousarray(np.asarray(inputs["L0_ffn_W1"], np.float32).T)
    b1 = np.ascontiguousarray(np.asarray(inputs["L0_ffn_b1"], np.float32))
    w2 = np.ascontiguousarray(np.asarray(inputs["L0_ffn_W2"], np.float32).T)
    b2 = np.ascontiguousarray(np.asarray(inputs["L0_ffn_b2"], np.float32))
    scl = np.asarray(inputs["L0_ffn_scale"], dtype=np.float32).reshape(1)

    nc = build_nc()
    L = 8192
    in_maps = []
    for c in range(8):
        b, t0 = c // 4, (c % 4) * T_OWN
        idx = np.arange(t0 - 64, t0 + T_OWN + 64) % L
        xh = np.ascontiguousarray(x[b][idx])
        xl = np.ascontiguousarray(x[b][np.r_[L - 32:L, 0:64]])
        in_maps.append({
            "xh": xh, "xl": xl, "wg": wg, "wv": wv, "w1": w1, "w2": w2,
            "b1": b1, "b2": b2, "gamma": gam, "beta": bet, "scal": scl,
        })
    res = run_bass_kernel_spmd(nc, in_maps, core_ids=list(range(8)))
    out = np.empty((2, L, D), np.float32)
    for c, r in enumerate(res.results):
        b, t0 = c // 4, (c % 4) * T_OWN
        out[b, t0:t0 + T_OWN] = np.asarray(r["out"])
    return out


# version 13
# speedup vs baseline: 1.0340x; 1.0065x over previous
import contextlib
import numpy as np

import concourse.bass as bass
import concourse.mybir as mybir
from concourse.alu_op_type import AluOpType
from concourse.bass_utils import run_bass_kernel_spmd

F32 = mybir.dt.float32
BF16 = mybir.dt.bfloat16
AF = mybir.ActivationFunctionType

D = 1024
KO = 8
H = 4096
MG = 32
T_OWN = 2048
HALF = 1024
EXT = 2176
VW = 1088
SHIFTS = [0, -32, -16, -4, -1, 1, 4, 16, 32]
NK = 9
EPS = 1e-5
FT = 256
NFT = T_OWN // FT

TE, SC, VE, GP, SY = range(5)
_QN = ["tensor", "scalar", "vector", "gpsimd", "sync"]

_DEBUG = False


NL = 8


class _Op:
    __slots__ = ("qi", "fn", "dma", "need", "deps", "val", "flushed", "tiny",
                 "sem")


class _Prog:

    def __init__(self, sems, dsems):
        self.sems = sems
        self.dsems = dsems
        self.q = [[] for _ in range(5)]
        self.cnt = [0] * 5
        self.dcnt = [[0] * NL for _ in range(5)]
        self.dseq = [0] * 5
        self.dlast = [[None] * NL for _ in range(5)]
        self.waited = [dict() for _ in range(5)]
        self.final_wait = None

    def emit(self, qi, fn, deps=(), dma=False, need=False, tiny=False):
        op = _Op()
        op.qi, op.fn, op.dma, op.need = qi, fn, dma, need
        op.deps, op.val, op.flushed = [], None, False
        op.tiny = tiny
        op.sem = None
        if dma:
            lane = self.dseq[qi] % NL
            self.dseq[qi] += 1
            self.dcnt[qi][lane] += 16
            op.sem = self.dsems[qi][lane]
            op.val = self.dcnt[qi][lane]
            op.need = True
            self.dlast[qi][lane] = op
        for d in deps:
            if d is None:
                continue
            if d.qi == qi and not d.dma and not d.tiny and qi != GP:
                continue
            if d.flushed:
                assert d.need, "dep on flushed op that was not need-marked"
            elif not d.dma:
                d.need = True
            op.deps.append(d)
        self.q[qi].append(op)
        return op

    def barrier(self):
        lasts = [(self.q[i][-1] if self.q[i] else None) for i in range(5)]
        dl = [op for qi in range(5) for op in self.dlast[qi] if op is not None]
        nops = []
        for qi in range(5):
            deps = [l for j, l in enumerate(lasts) if l is not None and j != qi]
            deps += dl
            nops.append(self.emit(qi, lambda e: e.nop(), deps=deps, need=True))
        return nops

    def flush_into(self, block):
        for qi in range(5):
            c = self.cnt[qi]
            for op in self.q[qi]:
                if op.dma:
                    continue
                if op.need:
                    c += 1
                op.val = c
            self.cnt[qi] = c
        pend = [list(self.q[i]) for i in range(5)]
        for i in range(5):
            self.q[i] = []
        prog = self

        def mk(qi):
            def section(e):
                for op in pend[qi]:
                    for d in op.deps:
                        w = prog.waited[qi]
                        sem = d.sem if d.dma else prog.sems[d.qi]
                        key = id(sem)
                        if w.get(key, -1) >= d.val:
                            continue
                        w[key] = d.val
                        e.wait_ge(sem, d.val)
                    inst = op.fn(e)
                    if op.need:
                        if isinstance(inst, (list, tuple)):
                            inst = inst[-1]
                        if op.dma:
                            inst.then_inc(op.sem, 16)
                        else:
                            inst.then_inc(prog.sems[qi], 1)
                    op.flushed = True
                if prog.final_wait is not None and prog.final_wait.qi == qi \
                        and prog.final_wait in pend[qi]:
                    fw = prog.final_wait
                    e.wait_ge(fw.sem if fw.dma else prog.sems[qi], fw.val)
            return section

        for qi, name in enumerate(_QN):
            getattr(block, name)(mk(qi))


def _ap(t, off, dims):
    return bass.AP(t, off, [list(d) for d in dims])


def build_nc():
    nc = bass.Bass("TRN2", target_bir_lowering=False)

    xh_d = nc.dram_tensor("xh", [EXT, D], F32, kind="ExternalInput")
    xl_d = nc.dram_tensor("xl", [96, D], F32, kind="ExternalInput")
    wg_d = nc.dram_tensor("wg", [NK * D, D], F32, kind="ExternalInput")
    wv_d = nc.dram_tensor("wv", [D, D], F32, kind="ExternalInput")
    w1_d = nc.dram_tensor("w1", [D, H], F32, kind="ExternalInput")
    w2_d = nc.dram_tensor("w2", [H, D], F32, kind="ExternalInput")
    b1_d = nc.dram_tensor("b1", [H], F32, kind="ExternalInput")
    b2_d = nc.dram_tensor("b2", [D], F32, kind="ExternalInput")
    gam_d = nc.dram_tensor("gamma", [D], F32, kind="ExternalInput")
    bet_d = nc.dram_tensor("beta", [D], F32, kind="ExternalInput")
    scl_d = nc.dram_tensor("scal", [1], F32, kind="ExternalInput")
    out_d = nc.dram_tensor("out", [T_OWN, D], F32, kind="ExternalOutput")

    ikind = "ExternalOutput" if _DEBUG else "Internal"
    x1sp = nc.dram_tensor("x1sp", [T_OWN, D], F32, kind=ikind)
    y2ls_dram = nc.dram_tensor("y2ls_dram", [16, D], BF16, kind=ikind)
    if _DEBUG:
        dbg_y1t = nc.dram_tensor("dbg_y1t", [128, KO * EXT], BF16,
                                 kind="ExternalOutput")
        dbg_v = nc.dram_tensor("dbg_v", [128, KO * VW], F32,
                               kind="ExternalOutput")
        dbg_acc = nc.dram_tensor("dbg_acc", [128, KO * HALF], F32,
                                 kind="ExternalOutput")

    with contextlib.ExitStack() as outer:
        E = outer.enter_context
        sems = [E(nc.semaphore(f"s_{n}")) for n in _QN]
        dsems = [None] * 5
        for qi in (SC, GP, SY):
            dsems[qi] = [E(nc.semaphore(f"d_{_QN[qi]}_{l}"))
                         for l in range(NL)]
        P = _Prog(sems, dsems)

        id_f = E(nc.sbuf_tensor("id_f", [128, 128], F32))
        scale_b = E(nc.sbuf_tensor("scale_b", [128, 1], F32))
        b1c = E(nc.sbuf_tensor("b1c", [128, 32], F32))
        b2c = E(nc.sbuf_tensor("b2c", [128, 8], F32))
        odd1T = E(nc.sbuf_tensor("odd1T", [128, 32], BF16))
        O_all = E(nc.sbuf_tensor("O_all", [128, 512], BF16))
        M2 = E(nc.sbuf_tensor("M2", [16, D], BF16))
        stats = E(nc.sbuf_tensor("stats", [128, 64], F32))
        bnb = E(nc.sbuf_tensor("bnb", [128, 12], F32))
        cst = E(nc.sbuf_tensor("cst", [128, 4], F32))
        cst_tok = [None]

        slot_last = [None] * 8
        slot_rr = [0]

        def rstd(src_t, nrows, deps):
            s = slot_rr[0]
            slot_rr[0] = (s + 1) % 8
            st = stats
            c0 = s * 8
            dps = list(deps)
            a1 = P.emit(VE, lambda e, t=src_t, n=nrows: e.bn_stats(
                bnb[0:n, 0:6], t[0:n, 0:512]), deps=dps, tiny=True)
            a2 = P.emit(VE, lambda e, t=src_t, n=nrows: e.bn_stats(
                bnb[0:n, 6:12], t[0:n, 512:1024]), deps=dps + [a1], tiny=True)
            sdep = [slot_last[s]] if slot_last[s] is not None else []
            b = P.emit(VE, lambda e, n=nrows, c0=c0: e.bn_aggr(
                st[0:n, c0:c0 + 2], bnb[0:n, 0:12]),
                deps=[a1, a2] + sdep, tiny=True)
            a4 = P.emit(SC, lambda e, n=nrows, c0=c0: e.activation(
                st[0:n, c0 + 2:c0 + 3], st[0:n, c0 + 1:c0 + 2], AF.Sqrt,
                bias=cst[0:n, 0:1]), deps=[b, cst_tok[0]], tiny=True)
            a5 = P.emit(VE, lambda e, n=nrows, c0=c0: e.reciprocal(
                st[0:n, c0 + 3:c0 + 4], st[0:n, c0 + 2:c0 + 3]),
                deps=[a4], tiny=True)
            a6 = P.emit(VE, lambda e, n=nrows, c0=c0: e.tensor_scalar(
                st[0:n, c0 + 4:c0 + 5], st[0:n, c0:c0 + 1],
                st[0:n, c0 + 3:c0 + 4], cst[0:n, 1:2],
                AluOpType.mult, AluOpType.mult),
                deps=[a5, b, cst_tok[0]], tiny=True)
            return st[0:nrows, c0 + 3:c0 + 4], st[0:nrows, c0 + 4:c0 + 5], a6, s

        def slot_done(s, op):
            slot_last[s] = op

        cross = {}

        with contextlib.ExitStack() as attn:
            A = attn.enter_context
            y1T = A(nc.sbuf_tensor("y1T", [128, KO * EXT], BF16))
            y2L = A(nc.sbuf_tensor("y2L", [16, D], F32))
            y2LT = A(nc.sbuf_tensor("y2LT", [128, 128], BF16))
            y2ls_sb = A(nc.sbuf_tensor("y2ls_sb", [16, D], BF16))
            acc = A(nc.sbuf_tensor("acc", [128, KO * HALF], F32))
            v_all = A(nc.sbuf_tensor("v_all", [128, KO * VW], F32))
            wgt = A(nc.sbuf_tensor("wgt", [128, 2 * KO * D], BF16))
            wvt = A(nc.sbuf_tensor("wvt", [128, KO * D], BF16))
            x0r = [A(nc.sbuf_tensor(f"x0r{i}", [128, D], F32)) for i in range(2)]
            g_buf = [A(nc.sbuf_tensor(f"gb{i}", [128, D], F32)) for i in range(2)]
            wring = [A(nc.sbuf_tensor(f"wr{i}", [128, D], F32)) for i in range(4)]
            vL = A(nc.sbuf_tensor("vL", [128, KO * 88], F32))
            accL = A(nc.sbuf_tensor("accL", [128, KO * 16], F32))
            y1LT = A(nc.sbuf_tensor("y1LT", [128, KO * 128], BF16))
            gl_sb = A(nc.sbuf_tensor("gl_sb", [128, 16], F32))
            gam_b = A(nc.sbuf_tensor("gam_b", [128, D], F32))
            bet_b = A(nc.sbuf_tensor("bet_b", [128, D], F32))
            xlt = A(nc.sbuf_tensor("xlt", [128, D], F32))
            w_z = g_buf[0]
            lwB = g_buf[1]

            p_va = A(nc.psum_tensor("p_va", [128, 512], F32))
            p_vb = A(nc.psum_tensor("p_vb", [128, 512], F32))
            p_g = [A(nc.psum_tensor(f"p_g{i}", [128, 1024], F32)) for i in range(2)]
            p_gl = A(nc.psum_tensor("p_gl", [128, 512], F32))
            p_tr = A(nc.psum_tensor("p_tr", [128, 512], F32))

            ce = P.emit(GP, lambda e: e.memset(cst[:, 0:1], EPS))
            ce2 = P.emit(GP, lambda e: e.memset(cst[:, 1:2], -1.0), deps=[ce])
            ce3 = P.emit(GP, lambda e: e.memset(cst[:, 2:3], 0.5), deps=[ce2])
            cst_tok[0] = P.emit(GP, lambda e: e.memset(cst[:, 3:4], 0.25),
                                deps=[ce3])
            idm = P.emit(GP, lambda e: e.memset(id_f[:, :], 0.0))
            ids = P.emit(GP, lambda e: e.affine_select(
                out=id_f[:, :], in_=id_f[:, :], compare_op=AluOpType.not_equal,
                fill=1.0, base=0, pattern=[[-1, 128]], channel_multiplier=1),
                deps=[idm])

            ld_sc = P.emit(SY, lambda e: e.dma_start(
                scale_b[:, :], _ap(scl_d, 0, [[0, 128], [1, 1]])), dma=True)
            ld_g = P.emit(SY, lambda e: e.dma_start(
                gam_b[:, :], _ap(gam_d, 0, [[0, 128], [1, D]])), dma=True)
            ld_b = P.emit(SY, lambda e: e.dma_start(
                bet_b[:, :], _ap(bet_d, 0, [[0, 128], [1, D]])), dma=True)
            ld_b1 = P.emit(SY, lambda e: e.dma_start(
                xlt[0:32, 0:128], _ap(b1_d, 0, [[128, 32], [1, 128]])), dma=True)
            ld_b2 = P.emit(SY, lambda e: e.dma_start(
                xlt[0:8, 256:384], _ap(b2_d, 0, [[128, 8], [1, 128]])), dma=True)

            t_b1 = P.emit(TE, lambda e: e.transpose(
                p_tr[:, 0:32], xlt[0:32, 0:128], id_f[0:32, 0:32]),
                deps=[ld_b1, ids])
            e_b1 = P.emit(VE, lambda e: e.tensor_copy(b1c[:, :], p_tr[:, 0:32]),
                          deps=[t_b1], need=True)
            t_b2 = P.emit(TE, lambda e: e.transpose(
                p_tr[:, 32:40], xlt[0:8, 256:384], id_f[0:8, 0:8]),
                deps=[ld_b2, e_b1])
            e_b2 = P.emit(VE, lambda e: e.tensor_copy(b2c[:, :], p_tr[:, 32:40]),
                          deps=[t_b2], need=True)
            ptr_free = e_b2

            wvt_ops = []
            for ko in range(KO):
                wvt_ops.append(P.emit(GP, lambda e, ko=ko: e.dma_start(
                    wvt[:, ko * D:(ko + 1) * D],
                    wv_d.ap()[ko * 128:(ko + 1) * 128, :]),
                    dma=True, need=True))
            wvt_done = wvt_ops[-1]

            mz = P.emit(GP, lambda e: e.memset(xlt[:, :], 0.0),
                        deps=[t_b1, t_b2])
            ld_xl = P.emit(SY, lambda e: e.dma_start(xlt[0:96, :], xl_d.ap()[:, :]),
                           dma=True, deps=[mz])
            r_ap, m_ap, rop, sl = rstd(xlt, 128, [ld_xl])
            nrm_xl = P.emit(SC, lambda e, r=r_ap, m=m_ap: e.activation(
                wring[2][:, :], xlt[:, :], AF.Identity, bias=m, scale=r),
                deps=[rop])
            slot_done(sl, nrm_xl)
            tl_last = None
            for hb in range(2):
                tro = []
                for j in range(4):
                    ko = hb * 4 + j
                    tro.append(P.emit(TE, lambda e, ko=ko, j=j: e.transpose(
                        p_tr[:, j * 128:(j + 1) * 128],
                        wring[2][:, ko * 128:(ko + 1) * 128], id_f[:, :]),
                        deps=[nrm_xl, ids, ptr_free]))
                ev = P.emit(VE, lambda e, hb=hb: e.tensor_copy(
                    _ap(y1LT, hb * 4 * 128, [[KO * 128, 128], [128, 4], [1, 128]]),
                    _ap(p_tr, 0, [[512, 128], [128, 4], [1, 128]])),
                    deps=[tro[-1]], need=True)
                ptr_free = ev
                tl_last = ev
            y1LT_done = tl_last
            wring_free = [None, None, tl_last, None]

            x0r_free = [None, None]
            y1t_ev = [None] * 17

            def p1_tile(i):
                nonlocal ptr_free
                sl_i = i % 2
                wsl = i % 4
                ld = P.emit(SY, lambda e, i=i, s=sl_i: e.dma_start(
                    x0r[s][:, :], xh_d.ap()[i * 128:(i + 1) * 128, :]),
                    dma=True, deps=[x0r_free[sl_i]])
                r_ap, m_ap, rop, sl = rstd(x0r[sl_i], 128, [ld])
                nrm = P.emit(SC, lambda e, s=sl_i, w=wsl, r=r_ap, m=m_ap:
                             e.activation(wring[w][:, :], x0r[s][:, :],
                                          AF.Identity, bias=m, scale=r),
                             deps=[rop, wring_free[wsl]])
                slot_done(sl, nrm)
                x0r_free[sl_i] = nrm
                last_ev = None
                for hb in range(2):
                    tro = []
                    for j in range(4):
                        ko = hb * 4 + j
                        tro.append(P.emit(TE, lambda e, w=wsl, ko=ko, j=j:
                                          e.transpose(
                            p_tr[:, j * 128:(j + 1) * 128],
                            wring[w][:, ko * 128:(ko + 1) * 128], id_f[:, :]),
                            deps=[nrm, ptr_free]))
                    ev = P.emit(VE, lambda e, hb=hb, i=i: e.tensor_copy(
                        _ap(y1T, hb * 4 * EXT + i * 128,
                            [[KO * EXT, 128], [EXT, 4], [1, 128]]),
                        _ap(p_tr, 0, [[512, 128], [128, 4], [1, 128]])),
                        deps=[tro[-1]], need=True)
                    ptr_free = ev
                    last_ev = ev
                wring_free[wsl] = last_ev
                y1t_ev[i] = last_ev

            for i in range(9):
                p1_tile(i)

            def xbar_wg(k, dep_free):
                pk = k % 2
                ops = []
                for ko in range(KO):
                    ops.append(P.emit(GP, lambda e, k=k, ko=ko, pk=pk: e.dma_start(
                        wgt[:, pk * KO * D + ko * D:pk * KO * D + (ko + 1) * D],
                        wg_d.ap()[k * D + ko * 128:k * D + (ko + 1) * 128, :]),
                        dma=True, deps=[dep_free], need=True))
                return ops[-1]

            acc_free = None
            accL_done = None
            tt_last = None
            v_evac = [None] * KO
            vL_evac = [None] * KO
            gbuf_free = [None, None]
            sig_last = [None, None]
            pva_free, pvb_free = None, None
            wg_free = [None, None]
            x1sp_ops = []
            o_last = od_last = st_ls = None
            m2g = []

            def v_emit(h, tt_gate):
                nonlocal pva_free, pvb_free
                chunks = [(0, 512), (512, 512), (1024, 64)]
                v_par = 0
                for dg in range(KO):
                    my_evac = None
                    for (coff, cw) in chunks:
                        ps = p_va if v_par == 0 else p_vb
                        psfree = pva_free if v_par == 0 else pvb_free
                        mms = []
                        for ko in range(KO):
                            mms.append(P.emit(TE, lambda e, ps=ps, ko=ko, dg=dg,
                                              h=h, coff=coff, cw=cw: e.matmul(
                                ps[:, 0:cw],
                                wvt[:, ko * D + dg * 128:ko * D + (dg + 1) * 128],
                                y1T[:, ko * EXT + 32 + h * HALF + coff:
                                    ko * EXT + 32 + h * HALF + coff + cw],
                                start=(ko == 0), stop=(ko == KO - 1)),
                                deps=([wvt_done, psfree, y1t_ev[min(
                                    16, (32 + h * HALF + coff + cw - 1) // 128)]]
                                      if ko == 0 else [])))
                        ev = P.emit(SC, lambda e, ps=ps, dg=dg, coff=coff, cw=cw:
                                    e.copy(v_all[:, dg * VW + coff:
                                                 dg * VW + coff + cw],
                                           ps[:, 0:cw]),
                                    deps=[mms[-1], tt_gate], need=True)
                        my_evac = ev
                        if v_par == 0:
                            pva_free = ev
                        else:
                            pvb_free = ev
                        v_par ^= 1
                    v_evac[dg] = my_evac
                    if h == 0:
                        ps = p_va if v_par == 0 else p_vb
                        psfree = pva_free if v_par == 0 else pvb_free
                        mms = []
                        for ko in range(KO):
                            mms.append(P.emit(TE, lambda e, ps=ps, ko=ko, dg=dg:
                                              e.matmul(
                                ps[:, 0:96],
                                wvt[:, ko * D + dg * 128:ko * D + (dg + 1) * 128],
                                y1LT[:, ko * 128:ko * 128 + 96],
                                start=(ko == 0), stop=(ko == KO - 1)),
                                deps=([y1LT_done, psfree] if ko == 0 else [])))
                        ev = P.emit(SC, lambda e, ps=ps, dg=dg: e.copy(
                            vL[:, dg * 88:(dg + 1) * 88], ps[:, 1:89]),
                            deps=[mms[-1]], need=True)
                        vL_evac[dg] = ev
                        if v_par == 0:
                            pva_free = ev
                        else:
                            pvb_free = ev
                        v_par ^= 1

            for h in range(2):
                if h == 0:
                    v_emit(0, None)
                    for i in range(9, 17):
                        p1_tile(i)

                xb = [xbar_wg(0, wg_free[0]), xbar_wg(1, wg_free[1])]
                for k in range(NK):
                    pk = k % 2
                    s_k = SHIFTS[k]
                    for dg in range(KO):
                        pg = dg % 2
                        mms = []
                        for ko in range(KO):
                            for nch in range(2):
                                mms.append(P.emit(TE, lambda e, pg=pg, pk=pk,
                                                  ko=ko, dg=dg, h=h, nch=nch:
                                                  e.matmul(
                                    p_g[pg][:, nch * 512:(nch + 1) * 512],
                                    wgt[:, pk * KO * D + ko * D + dg * 128:
                                        pk * KO * D + ko * D + (dg + 1) * 128],
                                    y1T[:, ko * EXT + 64 + h * HALF + nch * 512:
                                        ko * EXT + 64 + h * HALF +
                                        (nch + 1) * 512],
                                    start=(ko == 0), stop=(ko == KO - 1)),
                                    deps=([xb[pk], sig_last[pg],
                                           y1t_ev[8 if h == 0 else 16]]
                                          if (ko == 0 and nch == 0) else [])))
                        wg_free[pk] = mms[-1]
                        sig = P.emit(SC, lambda e, pg=pg: e.activation(
                            g_buf[pg][:, :], p_g[pg][:, :], AF.Sigmoid),
                            deps=[mms[-1], gbuf_free[pg]], need=True)
                        sig_last[pg] = sig
                        if k == 0:
                            t2 = P.emit(VE, lambda e, pg=pg, dg=dg, s=s_k:
                                        e.tensor_tensor(
                                acc[:, dg * HALF:(dg + 1) * HALF],
                                g_buf[pg][:, :],
                                v_all[:, dg * VW + 32 + s:
                                       dg * VW + 32 + s + HALF],
                                AluOpType.mult),
                                deps=[sig, v_evac[dg], acc_free])
                        else:
                            t1 = P.emit(VE, lambda e, pg=pg, dg=dg, s=s_k:
                                        e.tensor_tensor(
                                g_buf[pg][:, :], g_buf[pg][:, :],
                                v_all[:, dg * VW + 32 + s:
                                       dg * VW + 32 + s + HALF],
                                AluOpType.mult),
                                deps=[sig, v_evac[dg]])
                            t2 = P.emit(VE, lambda e, pg=pg, dg=dg:
                                        e.tensor_tensor(
                                acc[:, dg * HALF:(dg + 1) * HALF],
                                acc[:, dg * HALF:(dg + 1) * HALF],
                                g_buf[pg][:, :], AluOpType.add), deps=[t1])
                        tt_last = t2
                        gbuf_free[pg] = t2
                        if h == 0:
                            lm = []
                            for ko in range(KO):
                                lm.append(P.emit(TE, lambda e, pk=pk, ko=ko,
                                                 dg=dg: e.matmul(
                                    p_gl[:, 0:16],
                                    wgt[:, pk * KO * D + ko * D + dg * 128:
                                        pk * KO * D + ko * D + (dg + 1) * 128],
                                    y1LT[:, ko * 128 + 33:ko * 128 + 49],
                                    start=(ko == 0), stop=(ko == KO - 1)),
                                    deps=([accL_done, y1LT_done]
                                          if ko == 0 else [])))
                            sigL = P.emit(SC, lambda e: e.activation(
                                gl_sb[:, :], p_gl[:, 0:16], AF.Sigmoid),
                                deps=[lm[-1], accL_done], need=True)
                            pgl_free = sigL
                            if k == 0:
                                tL = P.emit(VE, lambda e, dg=dg, s=s_k:
                                            e.tensor_tensor(
                                    accL[:, dg * 16:(dg + 1) * 16], gl_sb[:, :],
                                    vL[:, dg * 88 + 32 + s:dg * 88 + 48 + s],
                                    AluOpType.mult),
                                    deps=[sigL, vL_evac[dg]], tiny=True)
                            else:
                                tLa = P.emit(VE, lambda e, dg=dg, s=s_k:
                                             e.tensor_tensor(
                                    gl_sb[:, :], gl_sb[:, :],
                                    vL[:, dg * 88 + 32 + s:dg * 88 + 48 + s],
                                    AluOpType.mult),
                                    deps=[sigL, vL_evac[dg]], tiny=True)
                                tL = P.emit(VE, lambda e, dg=dg: e.tensor_tensor(
                                    accL[:, dg * 16:(dg + 1) * 16],
                                    accL[:, dg * 16:(dg + 1) * 16],
                                    gl_sb[:, :], AluOpType.add),
                                    deps=[tLa, accL_done], tiny=True)
                            accL_done = tL
                    if k + 2 < NK:
                        xb[(k + 2) % 2] = xbar_wg(k + 2, wg_free[(k + 2) % 2])

                if h == 0:
                    evs = []
                    for hb in range(2):
                        tro = []
                        for j in range(4):
                            dg = hb * 4 + j
                            tro.append(P.emit(TE, lambda e, dg=dg, j=j:
                                              e.transpose(
                                p_tr[0:16, j * 128:(j + 1) * 128],
                                accL[:, dg * 16:(dg + 1) * 16], id_f[:, :]),
                                deps=[accL_done, ptr_free]))
                        ev = P.emit(VE, lambda e, hb=hb: e.tensor_copy(
                            _ap(xlt, hb * 512, [[D, 16], [128, 4], [1, 128]]),
                            _ap(p_tr, 0, [[512, 16], [128, 4], [1, 128]])),
                            deps=[tro[-1]], need=True, tiny=True)
                        ptr_free = ev
                        evs.append(ev)
                    accLnat = evs[-1]
                    ld_x0L = P.emit(SY, lambda e: e.dma_start(
                        x0r[0][0:16, :], xl_d.ap()[33:49, :]), dma=True,
                        deps=[x0r_free[0]])
                    r1, n1, rop1, sl1 = rstd(xlt, 16, [accLnat])
                    nL = P.emit(SC, lambda e, r=r1, m=n1: e.activation(
                        lwB[0:16, :], xlt[0:16, :], AF.Identity, bias=m, scale=r),
                        deps=[rop1, accLnat, gbuf_free[1]])
                    slot_done(sl1, nL)
                    P.emit(VE, lambda e: e.tensor_tensor(
                        lwB[0:16, :], lwB[0:16, :], gam_b[0:16, :],
                        AluOpType.mult), deps=[nL, ld_g])
                    zL2 = P.emit(VE, lambda e: e.tensor_tensor(
                        lwB[0:16, :], lwB[0:16, :], bet_b[0:16, :],
                        AluOpType.add), deps=[ld_b])
                    r2, n2, rop2, sl2 = rstd(lwB, 16, [zL2])
                    n2L = P.emit(SC, lambda e, r=r2, m=n2: e.activation(
                        xlt[0:16, :], lwB[0:16, :], AF.Identity, bias=m, scale=r),
                        deps=[rop2])
                    slot_done(sl2, n2L)
                    gbuf_free[1] = n2L
                    x1L = P.emit(VE, lambda e: e.tensor_tensor(
                        xlt[0:16, :], xlt[0:16, :], x0r[0][0:16, :],
                        AluOpType.add),
                        deps=[n2L, ld_x0L])
                    x0r_free[0] = x1L
                    r3, n3, rop3, sl3 = rstd(xlt, 16, [x1L])
                    y2L_op = P.emit(SC, lambda e, r=r3, m=n3: e.activation(
                        y2L[:, :], xlt[0:16, :], AF.Identity, bias=m, scale=r),
                        deps=[rop3, x1L])
                    slot_done(sl3, y2L_op)
                    tro = []
                    for ko in range(KO):
                        tro.append(P.emit(TE, lambda e, ko=ko: e.transpose(
                            p_tr[:, ko * 16:(ko + 1) * 16],
                            y2L[:, ko * 128:(ko + 1) * 128], id_f[0:16, 0:16]),
                            deps=[y2L_op, ptr_free]))
                    y2LT_op = P.emit(VE, lambda e: e.tensor_copy(
                        y2LT[:, :], p_tr[:, 0:128]), deps=[tro[-1]], need=True,
                        tiny=True)
                    ptr_free = y2LT_op
                    y2ls_op = P.emit(SC, lambda e: e.activation(
                        y2ls_sb[:, :], y2L[:, :], AF.Identity,
                        scale=scale_b[0:16, 0:1]), deps=[y2L_op, ld_sc])
                    st_ls = P.emit(SY, lambda e: e.dma_start(
                        y2ls_dram.ap(), y2ls_sb[:, :]), dma=True,
                        deps=[y2ls_op], need=True)
                    m2z = P.emit(GP, lambda e: e.memset(M2[:, :], 0.0))
                    for j in range(4):
                        for pm in range(4):
                            m2g.append(P.emit(GP, lambda e, j=j, pm=pm:
                                              e.dma_start(
                                M2[j * 4 + pm:j * 4 + pm + 1,
                                   j * 256:(j + 1) * 256],
                                _ap(y2ls_dram, (8 + 2 * j) * D + pm,
                                    [[1, 1], [4, 256]]),
                                allow_slow_non_contiguous=True),
                                dma=True, deps=[st_ls, m2z], need=True))
                    oz = P.emit(VE, lambda e: e.memset(O_all[:, :], 0.0),
                                deps=[y2LT_op], tiny=True)
                    for g in range(32):
                        pm, ko = g // 8, g % 8
                        o_last = P.emit(VE, lambda e, g=g, pm=pm, ko=ko:
                                        e.tensor_copy(
                            _ap(O_all, g * 16 + pm, [[512, 128], [4, 4]]),
                            _ap(y2LT, ko * 16 + 9, [[128, 128], [2, 4]])),
                            deps=[oz, y2LT_op], need=(g == 31))
                    for ko in range(KO):
                        od_last = P.emit(VE, lambda e, ko=ko: e.tensor_copy(
                            _ap(odd1T, ko * 4, [[32, 128], [1, 4]]),
                            _ap(y2LT, ko * 16 + 1, [[128, 128], [2, 4]])),
                            deps=[y2LT_op], need=(ko == KO - 1))
                    v_emit(1, tt_last)

                tt_final = tt_last
                tr_last = None
                wz_free = [gbuf_free[0], gbuf_free[1]]
                if h == 0:
                    ering = wring + x0r
                    ering_free = [wring_free[0], wring_free[1], wring_free[2],
                                  wring_free[3], x0r_free[0], x0r_free[1]]
                for i in range(8):
                    wsl = i % 6
                    tro0 = []
                    for j in range(4):
                        tro0.append(P.emit(TE, lambda e, i=i, j=j:
                                           e.transpose(
                            p_tr[:, j * 128:(j + 1) * 128],
                            acc[:, j * HALF + i * 128:
                                j * HALF + (i + 1) * 128],
                            id_f[:, :]),
                            deps=[tt_final, ptr_free]))
                    ev0 = P.emit(VE, lambda e, w=wsl: e.tensor_copy(
                        ering[w][:, 0:512], p_tr[:, 0:512]),
                        deps=[tro0[-1], ering_free[wsl]], need=True, tiny=True)
                    ptr_free = ev0
                    tro1 = []
                    for j in range(4):
                        dg = 4 + j
                        tro1.append(P.emit(TE, lambda e, dg=dg, i=i, j=j:
                                           e.transpose(
                            p_gl[:, j * 128:(j + 1) * 128],
                            acc[:, dg * HALF + i * 128:
                                dg * HALF + (i + 1) * 128],
                            id_f[:, :]),
                            deps=[tt_final, pgl_free]))
                    tr_last = tro1[-1]
                    ev1 = P.emit(SC, lambda e, w=wsl: e.copy(
                        ering[w][:, 512:1024], p_gl[:, 0:512]),
                        deps=[tro1[-1], ering_free[wsl]], need=True, tiny=True)
                    pgl_free = ev1
                    natacc = ev1
                    wz = g_buf[i % 2]
                    r1, n1, rop1, sl1 = rstd(ering[wsl], 128, [ev0, ev1])
                    nrm1 = P.emit(SC, lambda e, wz=wz, w=wsl, r=r1, m=n1:
                                  e.activation(
                        wz[:, :], ering[w][:, :], AF.Identity, bias=m, scale=r),
                        deps=[rop1, ev0, ev1, wz_free[i % 2]])
                    slot_done(sl1, nrm1)
                    P.emit(VE, lambda e, wz=wz: e.tensor_tensor(
                        wz[:, :], wz[:, :], gam_b[:, :], AluOpType.mult),
                        deps=[nrm1, ld_g])
                    z2 = P.emit(VE, lambda e, wz=wz: e.tensor_tensor(
                        wz[:, :], wz[:, :], bet_b[:, :], AluOpType.add),
                        deps=[ld_b])
                    r2, n2, rop2, sl2 = rstd(wz, 128, [z2])
                    nrm2 = P.emit(SC, lambda e, wz=wz, w=wsl, r=r2, m=n2:
                                  e.activation(
                        ering[w][:, :], wz[:, :], AF.Identity, bias=m, scale=r),
                        deps=[rop2])
                    slot_done(sl2, nrm2)
                    wz_free[i % 2] = nrm2
                    sp = P.emit(SY, lambda e, h=h, i=i, w=wsl: e.dma_start(
                        x1sp.ap()[h * HALF + i * 128:h * HALF + (i + 1) * 128, :],
                        ering[w][:, :]), dma=True, deps=[nrm2], need=True)
                    x1sp_ops.append(sp)
                    ering_free[wsl] = sp
                acc_free = tr_last
                gbuf_free[0] = wz_free[0]
                gbuf_free[1] = wz_free[1]

            spill_done = x1sp_ops[-1]
            if _DEBUG:
                P.emit(SY, lambda e: e.dma_start(dbg_y1t.ap(), y1T[:, :]),
                       dma=True, deps=[y1t_ev[16]], need=True)
                P.emit(SY, lambda e: e.dma_start(dbg_v.ap(), v_all[:, :]),
                       dma=True, deps=[v_evac[KO - 1]], need=True)
                P.emit(SY, lambda e: e.dma_start(dbg_acc.ap(), acc[:, :]),
                       dma=True, deps=[tt_last], need=True)
            bar = P.barrier()
            cross.update(m2g=m2g,
                         o_last=o_last, od_last=od_last, st_ls=st_ls,
                         e_b1=e_b1, e_b2=e_b2, spill_done=spill_done, bar=bar,
                         x1sp_ops=x1sp_ops)

            blk1 = attn.enter_context(nc.Block())
            P.flush_into(blk1)

        for s in range(8):
            slot_last[s] = None

        with contextlib.ExitStack() as ffn:
            B = ffn.enter_context
            w1t = B(nc.sbuf_tensor("w1t", [128, KO * H], BF16))
            w2t = B(nc.sbuf_tensor("w2t", [128, MG * D], BF16))
            h_bf = B(nc.sbuf_tensor("h_bf", [128, MG * FT], BF16))
            y2t = [B(nc.sbuf_tensor(f"y2t{i}", [128, KO * FT], BF16))
                   for i in range(2)]
            x1r = [B(nc.sbuf_tensor(f"x1r{i}", [128, D], F32)) for i in range(4)]
            wf = [B(nc.sbuf_tensor(f"wf{i}", [128, D], F32)) for i in range(4)]
            w_odt = B(nc.sbuf_tensor("w_odt", [128, KO * FT], F32))
            M1 = B(nc.sbuf_tensor("M1", [4, H], BF16))
            c_bf = B(nc.sbuf_tensor("c_bf", [4, FT], BF16))
            t_bf = B(nc.sbuf_tensor("t_bf", [16, FT], BF16))

            p_c = B(nc.psum_tensor("p_c", [4, 512], F32))
            p_t = B(nc.psum_tensor("p_t", [16, 512], F32))
            p_h = [B(nc.psum_tensor(f"p_h{i}", [128, 512], F32))
                   for i in range(2)]
            p_o = [B(nc.psum_tensor(f"p_o{i}", [128, 512], F32))
                   for i in range(2)]
            p_f = B(nc.psum_tensor("p_f", [128, 1024], F32))

            bar = cross["bar"]
            bsy, bgp = bar[SY], bar[GP]
            w1t_ops = []
            for g in range(8):
                w1t_ops.append(P.emit(GP, lambda e, g=g: e.dma_start(
                    _ap(w1t, g * 512, [[KO * H, 128], [H, KO], [1, 512]]),
                    _ap(w1_d, g * 512, [[H, 128], [128 * H, KO], [1, 512]])),
                    dma=True, deps=[bgp], need=True))
            w2t_ops = []
            for ko in range(MG):
                w2t_ops.append(P.emit(GP, lambda e, ko=ko: e.dma_start(
                    w2t[:, ko * D:(ko + 1) * D],
                    w2_d.ap()[ko * 128:(ko + 1) * 128, :]),
                    dma=True, deps=[bgp], need=True))
            m1z = P.emit(GP, lambda e: e.memset(M1[:, :], 0.0), deps=[bgp])
            m1_done = None
            for j in range(4):
                m1_done = P.emit(SY, lambda e, j=j: e.dma_start(
                    M1[j:j + 1, j * D:(j + 1) * D],
                    y2ls_dram.ap()[2 * j:2 * j + 1, :]),
                    dma=True, deps=[m1z, cross["st_ls"], bsy], need=True)

            ptr2_free = bar[VE]
            x1r_free = [None] * 4
            wf_free = [None] * 4
            y2t_free = [None, None]
            hbf_free = None
            codt_free = None
            gel_prev = [None, None]
            oev_prev = [None, None]
            prev_cev = None
            prev_tev = None
            cbf_free = None
            tbf_free = None
            out_ops = []

            def pro(i):
                sub_nrm = []
                adds = []
                for sub in range(2):
                    s = (2 * i + sub) % 4
                    wslot = (2 * i + sub) % 4
                    lda = P.emit(SY, lambda e, i=i, sub=sub, s=s: e.dma_start(
                        x1r[s][:, :],
                        x1sp.ap()[i * FT + sub * 128:i * FT + sub * 128 + 128, :]),
                        dma=True, deps=[x1r_free[s],
                                        cross["x1sp_ops"][2 * i + sub]])
                    ldx = P.emit(SY, lambda e, i=i, sub=sub, w=wslot:
                                 e.dma_start(
                        wf[w][:, :],
                        xh_d.ap()[64 + i * FT + sub * 128:
                                  64 + i * FT + sub * 128 + 128, :]),
                        dma=True, deps=[wf_free[wslot]])
                    xadd = P.emit(VE, lambda e, s=s, w=wslot: e.tensor_tensor(
                        x1r[s][:, :], x1r[s][:, :], wf[w][:, :],
                        AluOpType.add), deps=[lda, ldx])
                    adds.append(xadd)
                    r_ap, m_ap, rop, sl = rstd(x1r[s], 128, [xadd])
                    nrm = P.emit(SC, lambda e, s=s, w=wslot, r=r_ap, m=m_ap:
                                 e.activation(wf[w][:, :], x1r[s][:, :],
                                              AF.Identity, bias=m, scale=r),
                                 deps=[rop, xadd])
                    slot_done(sl, nrm)
                    sub_nrm.append((nrm, wslot))
                return {"lds": adds, "sub_nrm": sub_nrm}

            pro_d = [None] * NFT
            pro_d[0] = pro(0)
            for i in range(NFT):
                yp = i % 2
                lds = pro_d[i]["lds"]
                tr_ev = []
                for sub in range(2):
                    nrm, wslot = pro_d[i]["sub_nrm"][sub]
                    tro = []
                    for ko in range(KO):
                        tro.append(P.emit(TE, lambda e, w=wslot, ko=ko:
                                          e.transpose(
                            p_f[:, ko * 128:(ko + 1) * 128],
                            wf[w][:, ko * 128:(ko + 1) * 128], id_f[:, :]),
                            deps=[nrm, ptr2_free, y2t_free[yp]]))
                    ev = P.emit(SC, lambda e, yp=yp, sub=sub: e.copy(
                        _ap(y2t[yp], sub * 128,
                            [[KO * FT, 128], [FT, KO], [1, 128]]),
                        _ap(p_f, 0, [[1024, 128], [128, KO], [1, 128]])),
                        deps=[tro[-1]], need=True)
                    ptr2_free = ev
                    tr_ev.append(ev)
                    wf_free[wslot] = ev
                y2t_done = tr_ev[-1]
                cms = []
                for ko in range(KO):
                    cms.append(P.emit(TE, lambda e, yp=yp, ko=ko: e.matmul(
                        p_c[:, 0:FT], odd1T[:, ko * 4:(ko + 1) * 4],
                        y2t[yp][:, ko * FT:(ko + 1) * FT],
                        start=(ko == 0), stop=(ko == KO - 1)),
                        deps=([y2t_done, cross["od_last"], prev_cev]
                              if ko == 0 else [])))
                cev = P.emit(VE, lambda e: e.tensor_copy(c_bf[:, :],
                                                         p_c[:, 0:FT]),
                             deps=[cms[-1], cbf_free], need=True)
                prev_cev = cev
                gel_last = None
                hms = []
                for mg in range(MG):
                    hp = mg % 2
                    hms = []
                    for ko in range(KO):
                        hms.append(P.emit(TE, lambda e, hp=hp, yp=yp, ko=ko,
                                          mg=mg: e.matmul(
                            p_h[hp][:, 0:FT],
                            w1t[:, ko * H + mg * 128:ko * H + (mg + 1) * 128],
                            y2t[yp][:, ko * FT:(ko + 1) * FT],
                            start=(ko == 0), stop=False),
                            deps=([w1t_ops[mg // 4], y2t_done, gel_prev[hp]]
                                  if ko == 0 else [])))
                    hms.append(P.emit(TE, lambda e, hp=hp, mg=mg: e.matmul(
                        p_h[hp][:, 0:FT], M1[:, mg * 128:(mg + 1) * 128],
                        c_bf[:, :], start=False, stop=True),
                        deps=[cev, m1_done]))
                    gel = P.emit(SC, lambda e, hp=hp, mg=mg: e.activation(
                        h_bf[:, mg * FT:(mg + 1) * FT], p_h[hp][:, 0:FT],
                        AF.Gelu, bias=b1c[:, mg:mg + 1]),
                        deps=[hms[-1], cross["e_b1"], hbf_free], need=True)
                    gel_prev[hp] = gel
                    gel_last = gel
                cbf_free = hms[-1]
                if i + 1 < NFT:
                    pro_d[i + 1] = pro(i + 1)
                tms = []
                for g in range(MG):
                    tms.append(P.emit(TE, lambda e, g=g: e.matmul(
                        p_t[:, 0:FT], O_all[:, g * 16:(g + 1) * 16],
                        h_bf[:, g * FT:(g + 1) * FT],
                        start=(g == 0), stop=(g == MG - 1)),
                        deps=([gel_last, cross["o_last"], prev_tev]
                              if g == 0 else [])))
                tev = P.emit(VE, lambda e: e.tensor_copy(t_bf[:, :],
                                                         p_t[:, 0:FT]),
                             deps=[tms[-1], tbf_free], need=True)
                prev_tev = tev
                oev_last = None
                oms = []
                for dg in range(KO):
                    op_ = dg % 2
                    oms = []
                    for ko in range(MG):
                        oms.append(P.emit(TE, lambda e, op_=op_, ko=ko, dg=dg:
                                          e.matmul(
                            p_o[op_][:, 0:FT],
                            w2t[:, ko * D + dg * 128:ko * D + (dg + 1) * 128],
                            h_bf[:, ko * FT:(ko + 1) * FT],
                            start=(ko == 0), stop=False),
                            deps=([w2t_ops[ko], gel_last, oev_prev[op_]]
                                  if ko == 0 else [w2t_ops[ko]])))
                    oms.append(P.emit(TE, lambda e, op_=op_, dg=dg: e.matmul(
                        p_o[op_][:, 0:FT], M2[:, dg * 128:(dg + 1) * 128],
                        t_bf[:, :], start=False, stop=True),
                        deps=[tev] + cross["m2g"]))
                    oev = P.emit(SC, lambda e, op_=op_, dg=dg: e.activation(
                        w_odt[:, dg * FT:(dg + 1) * FT], p_o[op_][:, 0:FT],
                        AF.Identity, bias=b2c[:, dg:dg + 1]),
                        deps=[oms[-1], cross["e_b2"], codt_free], need=True)
                    oev_prev[op_] = oev
                    oev_last = oev
                hbf_free = oms[-1]
                tbf_free = oms[-1]
                y2t_free[yp] = hms[-1]
                tro = []
                for sub in range(2):
                    tro = []
                    for dg in range(KO):
                        tro.append(P.emit(TE, lambda e, dg=dg, sub=sub:
                                          e.transpose(
                            p_f[:, dg * 128:(dg + 1) * 128],
                            w_odt[:, dg * FT + sub * 128:
                                  dg * FT + sub * 128 + 128],
                            id_f[:, :]),
                            deps=[oev_last, ptr2_free]))
                    wslot = (2 * i + sub) % 4
                    ev = P.emit(VE, lambda e, w=wslot: e.tensor_copy(
                        wf[w][:, :], p_f[:, :]),
                        deps=[tro[-1], wf_free[wslot]], need=True, tiny=True)
                    ptr2_free = ev
                    r_ap, m_ap, rop, sl = rstd(wf[wslot], 128, [ev])
                    nrm = P.emit(SC, lambda e, w=wslot, r=r_ap, m=m_ap:
                                 e.activation(wf[w][:, :], wf[w][:, :],
                                              AF.Identity, bias=m, scale=r),
                                 deps=[rop, ev])
                    slot_done(sl, nrm)
                    xsl = (2 * i + sub) % 4
                    res = P.emit(VE, lambda e, w=wslot, xs=xsl: e.tensor_tensor(
                        wf[w][:, :], wf[w][:, :], x1r[xs][:, :], AluOpType.add),
                        deps=[nrm, lds[sub]])
                    st = P.emit(SY, lambda e, i=i, sub=sub, w=wslot:
                                e.dma_start(
                        out_d.ap()[i * FT + sub * 128:
                                   i * FT + sub * 128 + 128, :],
                        wf[w][:, :]), dma=True, deps=[res], need=True)
                    out_ops.append(st)
                    wf_free[wslot] = st
                    x1r_free[xsl] = res
                codt_free = tro[-1]

            P.final_wait = out_ops[-1]
            blk2 = ffn.enter_context(nc.Block())
            P.flush_into(blk2)
    return nc


def kernel(**inputs):
    x = np.asarray(inputs["L0_attn_x"], dtype=np.float32)
    wg0 = np.asarray(inputs["L0_attn_gate_W"], np.float32)
    wg = np.ascontiguousarray(
        wg0.reshape(NK, D, D).transpose(0, 2, 1)).reshape(NK * D, D)
    wv = np.ascontiguousarray(np.asarray(inputs["L0_attn_value_W"], np.float32).T)
    gam = np.ascontiguousarray(np.asarray(inputs["L0_attn_gamma"], np.float32))
    bet = np.ascontiguousarray(np.asarray(inputs["L0_attn_beta"], np.float32))
    w1 = np.ascontiguousarray(np.asarray(inputs["L0_ffn_W1"], np.float32).T)
    b1 = np.ascontiguousarray(np.asarray(inputs["L0_ffn_b1"], np.float32))
    w2 = np.ascontiguousarray(np.asarray(inputs["L0_ffn_W2"], np.float32).T)
    b2 = np.ascontiguousarray(np.asarray(inputs["L0_ffn_b2"], np.float32))
    scl = np.asarray(inputs["L0_ffn_scale"], dtype=np.float32).reshape(1)

    nc = build_nc()
    L = 8192
    in_maps = []
    for c in range(8):
        b, t0 = c // 4, (c % 4) * T_OWN
        idx = np.arange(t0 - 64, t0 + T_OWN + 64) % L
        xh = np.ascontiguousarray(x[b][idx])
        xl = np.ascontiguousarray(x[b][np.r_[L - 32:L, 0:64]])
        in_maps.append({
            "xh": xh, "xl": xl, "wg": wg, "wv": wv, "w1": w1, "w2": w2,
            "b1": b1, "b2": b2, "gamma": gam, "beta": bet, "scal": scl,
        })
    res = run_bass_kernel_spmd(nc, in_maps, core_ids=list(range(8)))
    out = np.empty((2, L, D), np.float32)
    for c, r in enumerate(res.results):
        b, t0 = c // 4, (c % 4) * T_OWN
        out[b, t0:t0 + T_OWN] = np.asarray(r["out"])
    return out
